# version 19
# baseline (speedup 1.0000x reference)
import numpy as np
import ml_dtypes
import concourse.bass as bass
import concourse.mybir as mybir
from concourse.bass_utils import run_bass_kernel_spmd

F32 = mybir.dt.float32
BF16 = mybir.dt.bfloat16
AF = mybir.ActivationFunctionType
ALU = mybir.AluOpType

NCORES = 8
SEQ_PER_CORE = 2
L = 2048
DM = 1024
DI = 2048
DS = 128
NH = 32
HD = 64
CD = 2304
DTP = 32
ZDT = DI + DTP
FF = 4096
T = 512
TT = 4
NSC_PER_SEQ = L // T
Q = 128
NCC_XBC = CD // 128
NCC_XS = DI // 128
NKC_DM = DM // 128
NKC_DI = DI // 128
NCC_FF = FF // 128
WH = 4608

ENGS = ("sync", "tensor", "vector", "scalar", "gpsimd")


class Ctr:
    __slots__ = ("v",)

    def __init__(self):
        self.v = 0

    def take(self, n=1):
        self.v += n
        return self.v


class Prog:

    def __init__(self, nc):
        self.nc = nc
        self.q = {e: [] for e in ENGS}
        self.bar = nc.alloc_semaphore("bar")
        self.nbar = 0
        self.esem = {e: nc.alloc_semaphore(f"sig_{e}") for e in ENGS}
        self.opctr = {e: Ctr() for e in ENGS}
        self.dsem = {e: nc.alloc_semaphore(f"dma_{e}") for e in ("sync", "scalar", "gpsimd")}
        self.dctr = {e: Ctr() for e in ("sync", "scalar", "gpsimd")}
        self.stsem = nc.alloc_semaphore("st")
        self.stctr = Ctr()
        self.wsem = [nc.alloc_semaphore("w0"), nc.alloc_semaphore("w1")]
        self.wctr = [Ctr(), Ctr()]
        self.rbsem = nc.alloc_semaphore("rb")
        self.rbctr = Ctr()
        self.rsem = [nc.alloc_semaphore("r0"), nc.alloc_semaphore("r1")]
        self.rctr = [Ctr(), Ctr()]
        self.xssem = [nc.alloc_semaphore("xs0"), nc.alloc_semaphore("xs1")]
        self.xsctr = [Ctr(), Ctr()]
        self.ynsem = [nc.alloc_semaphore(f"yn{i}") for i in range(4)]
        self.ynctr = [Ctr() for _ in range(4)]
        self.hsem = [nc.alloc_semaphore("hx0"), nc.alloc_semaphore("hx1")]
        self.hctr = [Ctr(), Ctr()]
        self.lsem = [nc.alloc_semaphore("l0"), nc.alloc_semaphore("l1")]
        self.lctr = [Ctr(), Ctr()]
        self.xlsem = [nc.alloc_semaphore("xl0"), nc.alloc_semaphore("xl1")]
        self.xlctr = [Ctr(), Ctr()]
        self.s2sem = [nc.alloc_semaphore("s0"), nc.alloc_semaphore("s1")]
        self.s2ctr = [Ctr(), Ctr()]

    def on(self, eng, fn):
        self.q[eng].append(fn)

    def dp(self, eng, fn):
        prev = self.opctr[eng].v
        tgt = self.opctr[eng].take(1)
        sem = self.esem[eng]

        def f(e, fn=fn, prev=prev, sem=sem):
            if prev > 0:
                e.wait_ge(sem, prev)
            fn(e).then_inc(sem, 1)
        self.on(eng, f)
        return (sem, tgt)

    def dma(self, qeng, out, in_, transpose=False, after_own=False):
        tgt = self.dctr[qeng].take(16)
        sem = self.dsem[qeng]
        own = self.opctr[qeng].v if after_own else 0
        osem = self.esem.get(qeng)

        def f(e, out=out, in_=in_, transpose=transpose, sem=sem, own=own, osem=osem):
            if own > 0:
                e.wait_ge(osem, own)
            e.dma_start(out=out, in_=in_, transpose=transpose).then_inc(sem, 16)
        self.on(qeng, f)
        return (self.dsem[qeng], tgt)

    def dma_sem(self, qeng, sem, ctr, out, in_, after_own=False, transpose=False,
                after_sem=0):
        tgt = ctr.take(16)
        own = self.opctr[qeng].v if after_own else 0
        osem = self.esem.get(qeng)

        def f(e, out=out, in_=in_, sem=sem, own=own, osem=osem, transpose=transpose,
              after_sem=after_sem):
            if own > 0:
                e.wait_ge(osem, own)
            if after_sem > 0:
                e.wait_ge(sem, after_sem)
            e.dma_start(out=out, in_=in_, transpose=transpose).then_inc(sem, 16)
        self.on(qeng, f)
        return (sem, tgt)

    def store_out(self, qeng, out, in_):
        tgt = self.stctr.take(16)

        def f(e, out=out, in_=in_):
            e.dma_start(out=out, in_=in_).then_inc(self.stsem, 16)
        self.on(qeng, f)
        return tgt

    def sig(self, eng):
        return (self.esem[eng], self.opctr[eng].v)

    def wait(self, eng, *sigs):
        for s in sigs:
            if s is None:
                continue
            sem, tgt = s
            if tgt <= 0:
                continue

            def f(e, sem=sem, tgt=tgt):
                e.wait_ge(sem, tgt)
            self.on(eng, f)

    def wait_dma(self, eng, sig_):
        self.wait(eng, sig_)

    def barrier(self):
        self.nbar += 1
        k = self.nbar
        dtargets = {e: self.dctr[e].v for e in self.dctr}
        otargets = {e: self.opctr[e].v for e in ENGS}
        st_t = self.stctr.v
        w_t = (self.wctr[0].v, self.wctr[1].v)
        rb_t = self.rbctr.v
        r_t = (self.rctr[0].v, self.rctr[1].v)
        xs_t = (self.xsctr[0].v, self.xsctr[1].v)
        yn_t = tuple(c.v for c in self.ynctr)
        h_t = (self.hctr[0].v, self.hctr[1].v)
        l_t = (self.lctr[0].v, self.lctr[1].v)
        s2_t = (self.s2ctr[0].v, self.s2ctr[1].v)
        xl_t = (self.xlctr[0].v, self.xlctr[1].v)
        for eng in ENGS:
            def f(e, eng=eng, k=k, dtargets=dtargets, otargets=otargets,
                  st_t=st_t, w_t=w_t, rb_t=rb_t, r_t=r_t, xs_t=xs_t, yn_t=yn_t, h_t=h_t,
                  l_t=l_t, s2_t=s2_t, xl_t=xl_t):
                if eng in dtargets and dtargets[eng] > 0:
                    e.wait_ge(self.dsem[eng], dtargets[eng])
                if eng == "gpsimd" and st_t > 0:
                    e.wait_ge(self.stsem, st_t)
                if eng == "scalar":
                    if w_t[0] > 0:
                        e.wait_ge(self.wsem[0], w_t[0])
                    if w_t[1] > 0:
                        e.wait_ge(self.wsem[1], w_t[1])
                    if rb_t > 0:
                        e.wait_ge(self.rbsem, rb_t)
                    for si in (0, 1):
                        if r_t[si] > 0:
                            e.wait_ge(self.rsem[si], r_t[si])
                if eng == "sync":
                    for si in (0, 1):
                        if xs_t[si] > 0:
                            e.wait_ge(self.xssem[si], xs_t[si])
                        if s2_t[si] > 0:
                            e.wait_ge(self.s2sem[si], s2_t[si])
                    for si in (0, 1):
                        if h_t[si] > 0:
                            e.wait_ge(self.hsem[si], h_t[si])
                    for si in range(4):
                        if yn_t[si] > 0:
                            e.wait_ge(self.ynsem[si], yn_t[si])
                if eng == "gpsimd":
                    for si in (0, 1):
                        if l_t[si] > 0:
                            e.wait_ge(self.lsem[si], l_t[si])
                        if xl_t[si] > 0:
                            e.wait_ge(self.xlsem[si], xl_t[si])
                if otargets[eng] > 0:
                    e.wait_ge(self.esem[eng], otargets[eng])
                e.drain()
                e.sem_inc(self.bar, 1)
                e.wait_ge(self.bar, 5 * k)
            self.on(eng, f)

    def finalize_block(self):
        nc = self.nc
        with nc.Block() as block:
            @block.sync
            def _(e):
                for f in self.q["sync"]:
                    f(e)

            @block.tensor
            def _(e):
                for f in self.q["tensor"]:
                    f(e)

            @block.vector
            def _(e):
                for f in self.q["vector"]:
                    f(e)

            @block.scalar
            def _(e):
                for f in self.q["scalar"]:
                    f(e)

            @block.gpsimd
            def _(e):
                for f in self.q["gpsimd"]:
                    f(e)


def build_program(n_sc=None, dbg=False):
    nc = bass.Bass("TRN2", target_bir_lowering=False)
    P = Prog(nc)

    x_d = nc.dram_tensor("x_in", [SEQ_PER_CORE, L, DM], F32, kind="ExternalInput").ap()
    wzdt_d = nc.dram_tensor("wzdt_t", [DM, ZDT], F32, kind="ExternalInput").ap()
    wxbc_d = nc.dram_tensor("wxbc_t", [DM, CD], F32, kind="ExternalInput").ap()
    convdiag_d = nc.dram_tensor("convdiag", [NCC_XBC, 128, 4 * 128], F32, kind="ExternalInput").ap()
    convb_d = nc.dram_tensor("convb_cols", [128, NCC_XBC], F32, kind="ExternalInput").ap()
    dtb_d = nc.dram_tensor("dtb_rep", [128, NH], F32, kind="ExternalInput").ap()
    alog_d = nc.dram_tensor("alog_rep", [128, NH], F32, kind="ExternalInput").ap()
    drep_d = nc.dram_tensor("d_rep", [128, NH], F32, kind="ExternalInput").ap()
    normw_d = nc.dram_tensor("normw_rep", [128, DI], F32, kind="ExternalInput").ap()
    b1_d = nc.dram_tensor("b1_cols", [128, NCC_FF], F32, kind="ExternalInput").ap()
    b2_d = nc.dram_tensor("b2_rep", [128, DM], F32, kind="ExternalInput").ap()
    wout_d = nc.dram_tensor("wout_t", [DI, DM], F32, kind="ExternalInput").ap()
    w1_d = nc.dram_tensor("w1_t", [DM, FF], F32, kind="ExternalInput").ap()
    w2_d = nc.dram_tensor("w2_t", [FF, DM], F32, kind="ExternalInput").ap()
    ident_d = nc.dram_tensor("ident128", [128, 128], F32, kind="ExternalInput").ap()
    identb_d = nc.dram_tensor("ident128b", [128, 128], BF16, kind="ExternalInput").ap()
    lt_d = nc.dram_tensor("lt128", [128, 128], F32, kind="ExternalInput").ap()

    out_d = nc.dram_tensor("out", [SEQ_PER_CORE, L, DM], F32, kind="ExternalOutput").ap()
    if dbg:
        dbgf_d = nc.dram_tensor("dbgf", [128, 16384], F32, kind="ExternalOutput").ap()
        dbgb_d = nc.dram_tensor("dbgb", [128, 32768], BF16, kind="ExternalOutput").ap()
        _dbg_state = {"f": 0, "b": 0, "taps": {}}

    def tap(name, src_ap, kind="f"):
        if not dbg:
            return
        w = src_ap.shape[-1] if len(src_ap.shape) == 2 else int(np.prod(src_ap.shape[1:]))
        st = _dbg_state[kind]
        dst = (dbgf_d if kind == "f" else dbgb_d)[0:src_ap.shape[0], st:st + w]
        _dbg_state["taps"][name] = (kind, st, w, src_ap.shape[0])
        _dbg_state[kind] = st + w
        P.barrier()
        P.dma("gpsimd", dst, src_ap)
        P.barrier()

    build_program.taps = None

    scr_wzdt = nc.dram_tensor("scr_wzdt", [NKC_DM, 128, ZDT], BF16, kind="Internal").ap()
    scr_wxbc = nc.dram_tensor("scr_wxbc", [NCC_XBC, NKC_DM, 128, 128], BF16, kind="Internal").ap()
    scr_conv = nc.dram_tensor("scr_conv", [NCC_XBC, 4, 128, 128], BF16, kind="Internal").ap()
    scr_wout = nc.dram_tensor("scr_wout", [NKC_DI, 128, DM], BF16, kind="Internal").ap()
    scr_t1 = nc.dram_tensor("scr_t1", [NCC_FF, NKC_DM, 128, 128], BF16, kind="Internal").ap()
    scr_t2 = nc.dram_tensor("scr_t2", [NCC_FF, 128, DM], BF16, kind="Internal").ap()
    scr_ac = nc.dram_tensor("scr_ac", [TT, NH * Q], F32, kind="Internal").ap()

    A = lambda name, shape, dt: nc.alloc_sbuf_tensor(name, shape, dt).ap()

    c_ident = A("c_ident", [128, 128], F32)
    c_identb = A("c_identb", [128, 128], BF16)
    c_lt = A("c_lt", [128, 128], F32)
    c_convb = A("c_convb", [128, NCC_XBC], F32)
    c_b1 = A("c_b1", [128, NCC_FF], F32)
    c_dtb = A("c_dtb", [128, NH], F32)
    c_alog = A("c_alog", [128, NH], F32)
    c_arep = A("c_arep", [128, NH], F32)
    c_drep = A("c_drep", [128, NH], F32)
    c_normw = A("c_normw", [128, DI], F32)
    c_b2 = A("c_b2", [128, DM], F32)
    c_eps = A("c_eps", [128, 1], F32)
    c_one = A("c_one", [128, 1], F32)
    c_onesrow = A("c_onesrow", [1, 128], F32)
    tern_cols = A("tern_cols", [128, 4], F32)
    run_abs = A("run_abs", [128, 2], F32)
    tmp_col = A("tmp_col", [128, 2], F32)
    row4 = A("row4", [1, 4], F32)

    s_all = A("s_all", [128, DI], F32)
    xbcraw = A("xbcraw", [128, NCC_XBC * (T + 3)], BF16)

    x_sb = A("x_sb", [128, 1024], F32)
    x2_sb = A("x2_sb", [128, TT * 1024], F32)
    arena_bf = A("arena_bf", [128, 24576], BF16)
    arena_f = A("arena_f", [128, 6144], F32)
    warena = A("warena", [128, 2 * WH], BF16)
    e_bf = A("e_bf", [128, 2048], BF16)
    mt_bf = A("mt_bf", [128, 2048], BF16)
    xs_tm = A("xs_tm", [128, 2 * DI], BF16)
    b_tm = A("b_tm", [128, 2 * DS], BF16)
    xdt_bf = A("xdt_bf", [128, DI], BF16)
    wxdt_bf = A("wxdt_bf", [128, DI], BF16)
    s_bf = A("s_bf", [128, DI], BF16)
    ynfm = A("ynfm", [128, NCC_XS * T], BF16)
    ynbf_tm = A("ynbf_tm", [128, DI], BF16)
    dt_tm = A("dt_tm", [128, TT * NH], F32)
    bfc = A("b_fm", [128, T], BF16)
    cfc = A("c_fm", [128, T], BF16)
    acum4 = A("acum4", [128, TT * NH], F32)
    acum_fm = A("acum_fm", [32, 128], F32)
    gm4 = A("gm4", [128, TT * 128], F32)
    cq4 = A("cq4", [128, TT * NH], F32)
    halo_sb = A("halo_sb", [128, NCC_XBC * 3], BF16)
    clast_rep = A("clast_rep", [128, NH], F32)
    wdec = A("wdec", [128, NH], F32)
    bns = A("bns", [128, 48], F32)
    stat2 = A("stat2", [128, 8], F32)
    rcol = A("rcol", [128, 8], F32)
    lnstats = A("lnstats", [128, 2 * TT], F32)
    msq = A("msq", [128, 1], F32)
    dtu = A("dtu", [128, NH], F32)
    dtt = A("dtt", [128, NH], F32)

    xs_fm = arena_bf[:, 0:8192]
    sz_tm = arena_bf[:, 8192:16384]
    h1_fm = arena_bf[:, 16384:20480]
    h1bf_tm = arena_bf[:, 20480:24576]
    hg_fm = arena_bf[:, 0:16384]
    h2_fm = arena_bf[:, 16384:20480]
    h2bf_tm = arena_bf[:, 20480:24576]
    mt_bf2 = arena_bf[:, 16384:18432]
    xdt_bf2 = arena_bf[:, 18432:20480]
    sq_scr = arena_bf[:, 20480:22528]
    e_bf2 = arena_bf[:, 22528:24576]

    r_half = arena_f[:, 0:2048]
    r_slot = (r_half, None)
    y_sb = arena_f[:, 2048:4096]
    vc_sb = arena_f[:, 4096:5120]
    stmp = arena_f[:, 5120:6144]
    h1t_sb = arena_f[:, 0:1024]
    at_sb = arena_f[:, 1024:2048]
    f_sb = arena_f[:, 2048:3072]
    o_sb = arena_f[:, 3072:4096]
    stg_f = arena_f[:, 0:4096]
    s1_f = x2_sb[:, 0:4096]
    s2_f = ynfm.bitcast(F32)[:, 0:4096]
    stg_bf = arena_bf[:, 0:4096]
    r_slot = (r_half, xbcraw.bitcast(F32)[:, 0:2048])

    pbw = [nc.alloc_psum_tensor(f"pbw{i}", [128, 1024], F32).ap() for i in range(4)]
    pb = []
    for i in range(4):
        pb.append(pbw[i][:, 0:512])
        pb.append(pbw[i][:, 512:1024])

    def mm(out, lhsT, rhs, start, stop):
        def f(e, out=out, lhsT=lhsT, rhs=rhs, start=start, stop=stop):
            e.matmul(out=out, lhsT=lhsT, rhs=rhs, start=start, stop=stop)
        P.on("tensor", f)

    def mm_inc(out, lhsT, rhs, start, stop):
        tgt = P.opctr["tensor"].take(1)
        sem = P.esem["tensor"]

        def f(e, out=out, lhsT=lhsT, rhs=rhs, start=start, stop=stop, sem=sem):
            e.matmul(out=out, lhsT=lhsT, rhs=rhs, start=start, stop=stop).then_inc(sem, 1)
        P.on("tensor", f)
        return (sem, tgt)

    def tr_inc(out, in_, identity):
        tgt = P.opctr["tensor"].take(1)
        sem = P.esem["tensor"]

        def f(e, out=out, in_=in_, identity=identity, sem=sem):
            e.transpose(out=out, in_=in_, identity=identity).then_inc(sem, 1)
        P.on("tensor", f)
        return (sem, tgt)

    def v(fn):
        return P.dp("vector", fn)

    def a(fn):
        return P.dp("scalar", fn)

    def g(fn):
        return P.dp("gpsimd", fn)

    pbsig = [None] * 8

    def pb_wait(*banks):
        for b in set(banks):
            if pbsig[b] is not None:
                P.wait("tensor", pbsig[b])

    def pb_free(sig, *banks):
        for b in banks:
            pbsig[b] = sig

    def emit_ln_stats(src, sl=0):
        b0 = sl * 12
        s0 = sl * 2
        v(lambda e: e.bn_stats(out=bns[:, b0:b0 + 6], in_=src[:, 0:512]))
        v(lambda e: e.bn_stats(out=bns[:, b0 + 6:b0 + 12], in_=src[:, 512:1024]))
        sv = v(lambda e: e.bn_aggr(out=stat2[:, s0:s0 + 2], in_=bns[:, b0:b0 + 12]))
        P.wait("scalar", sv)
        a(lambda e: e.activation(out=rcol[:, s0:s0 + 1], in_=stat2[:, s0 + 1:s0 + 2],
                                 func=AF.Ln, bias=c_eps[:]))
        sa = a(lambda e: e.activation(out=rcol[:, s0:s0 + 1], in_=rcol[:, s0:s0 + 1],
                                      func=AF.Exp, scale=-0.5))
        P.wait("vector", sa)
        v(lambda e: e.scalar_tensor_tensor(out=rcol[:, s0 + 1:s0 + 2],
                                           in0=stat2[:, s0:s0 + 1], scalar=-1.0,
                                           in1=rcol[:, s0:s0 + 1], op0=ALU.mult, op1=ALU.mult))

    for dst, src in ((c_ident, ident_d), (c_identb, identb_d), (c_lt, lt_d), (c_convb, convb_d),
                     (c_b1, b1_d), (c_dtb, dtb_d), (c_alog, alog_d),
                     (c_drep, drep_d), (c_normw, normw_d), (c_b2, b2_d)):
        P.dma("gpsimd", dst[:], src)
    v(lambda e: e.memset(c_eps[:], 1e-5))
    v(lambda e: e.memset(c_one[:], 1.0))
    v(lambda e: e.memset(c_onesrow[:], 1.0))
    v(lambda e: e.memset(s_all[:], 0.0))
    v(lambda e: e.memset(run_abs[:], 0.0))
    v(lambda e: e.memset(halo_sb[:], 0.0))
    P.barrier()

    sa = a(lambda e: e.activation(out=c_arep[:], in_=c_alog[:], func=AF.Exp))
    P.wait("vector", sa)
    v(lambda e: e.tensor_scalar_mul(out=c_arep[:], in0=c_arep[:], scalar1=-1.0))

    cast_sig = {}

    def cdma(dst, src):
        return P.dma_sem("gpsimd", P.lsem[0], P.lctr[0], dst, src)

    cast_sig["zdt"] = cdma(scr_wzdt, wzdt_d.rearrange("(k p) f -> k p f", k=NKC_DM))
    for g3 in range(6):
        for i3 in range(3):
            cc = g3 * 3 + i3
            s = cdma(scr_wxbc[cc],
                     wxbc_d[:, cc * 128:(cc + 1) * 128]
                     .rearrange("(k p) f -> k p f", k=NKC_DM))
        cast_sig[("xbc", g3)] = s
    for gi2, (c0, ncc) in enumerate(((0, 8), (8, 8), (16, 2))):
        for i3 in range(ncc):
            cc = c0 + i3
            s = cdma(scr_conv[cc], convdiag_d[cc].rearrange("p (k f) -> k p f", k=4))
        cast_sig[("conv", gi2)] = s
    cast_sig["wout"] = cdma(scr_wout, wout_d.rearrange("(k p) f -> k p f", k=NKC_DI))

    gsig = {}
    p1_items = [(w1_d[kc * 128:(kc + 1) * 128, q * 1024:(q + 1) * 1024], 0)
                for kc in range(NKC_DM) for q in range(4)]
    p1_items += [(w2_d[kc * 128:(kc + 1) * 128, :], 1) for kc in range(NCC_FF)]
    p1_slots = [arena_f[:, i * 1024:(i + 1) * 1024] for i in range(4)]
    p1 = {"i": 0, "cons": {}, "add": {}}

    def drain_pass1(n):
        for _ in range(n):
            i = p1["i"]
            if i >= len(p1_items):
                return
            src, col = p1_items[i]
            sl = i % 4
            if i >= 4:
                P.wait("sync", p1["cons"][i - 4])
            ld = P.dma_sem("sync", P.lsem[1], P.lctr[1], p1_slots[sl], src)
            P.wait_dma("scalar", ld)
            if i >= 2:
                P.wait("scalar", p1["add"][i - 2])
            sa = a(lambda e, sl=sl, i=i: e.activation(
                out=p1_slots[sl], in_=p1_slots[sl], func=AF.Abs,
                accum_out=tmp_col[:, i % 2:i % 2 + 1]))
            p1["cons"][i] = sa
            P.wait("vector", sa)
            p1["add"][i] = v(lambda e, col=col, i=i: e.tensor_add(
                out=run_abs[:, col:col + 1],
                in0=run_abs[:, col:col + 1], in1=tmp_col[:, i % 2:i % 2 + 1]))
            p1["i"] = i + 1
        if p1["i"] == len(p1_items):
            gsig["p1_done"] = p1["cons"][len(p1_items) - 1]

    def emit_thr():
        P.wait("tensor", p1["add"][len(p1_items) - 1])
        pb_wait(0)
        spe = mm_inc(pb[0][0:1, 0:2], c_one[:], run_abs[:, 0:2], True, True)
        P.wait("vector", spe)
        sv_r4 = v(lambda e: e.tensor_copy(out=row4[:, 0:2], in_=pb[0][0:1, 0:2]))
        pb_free(sv_r4, 0)
        v(lambda e: e.tensor_scalar_mul(out=row4[:, 0:1], in0=row4[:, 0:1],
                                        scalar1=1.0 / (FF * DM)))
        v(lambda e: e.tensor_scalar_mul(out=row4[:, 1:2], in0=row4[:, 1:2],
                                        scalar1=1.0 / (FF * DM)))
        v(lambda e: e.tensor_scalar_max(out=row4[:, 0:2], in0=row4[:, 0:2], scalar1=1e-8))
        v(lambda e: e.tensor_scalar_mul(out=row4[:, 0:2], in0=row4[:, 0:2], scalar1=0.5))
        v(lambda e: e.tensor_copy(out=row4[:, 2:3], in_=row4[:, 1:2]))
        v(lambda e: e.tensor_scalar_mul(out=row4[:, 1:2], in0=row4[:, 0:1], scalar1=-1.0))
        sv = v(lambda e: e.tensor_scalar_mul(out=row4[:, 3:4], in0=row4[:, 2:3], scalar1=-1.0))
        P.wait("tensor", sv)
        pb_wait(1)
        spe2 = mm_inc(pb[1][:, 0:4], c_onesrow[:], row4[:, 0:4], True, True)
        P.wait("vector", spe2)
        sv_tc = v(lambda e: e.tensor_copy(out=tern_cols[:], in_=pb[1][:, 0:4]))
        pb_free(sv_tc, 1)
        gsig["tern_cols"] = sv_tc

    p2_items = [(w1_d[kc * 128:(kc + 1) * 128, q * 1024:(q + 1) * 1024], 0, 1,
                 scr_t1[q * 8:(q + 1) * 8, kc], True)
                for kc in range(NKC_DM) for q in range(4)]
    p2_items += [(w2_d[kc * 128:(kc + 1) * 128, :], 2, 3, scr_t2[kc], False)
                 for kc in range(NCC_FF)]
    p2_lslots = [x2_sb[:, i * 1024:(i + 1) * 1024] for i in range(4)]
    p2_s1 = (xbcraw[:, 4096:5120], xbcraw[:, 5120:6144])
    p2_s2 = (xbcraw[:, 6144:7168], xbcraw[:, 7168:8192])
    p2 = {"i": 0, "cons": {}, "add": {}, "st": {}}

    def drain_pass2(n):
        for _ in range(n):
            i = p2["i"]
            if i >= len(p2_items):
                return
            src, thr_i, nthr_i, dst, is_t1 = p2_items[i]
            sl = i % 4
            s12 = i % 2
            if i >= 4:
                P.wait("gpsimd", p2["cons"][i - 4])
            ld = P.dma_sem("gpsimd", P.lsem[0], P.lctr[0], p2_lslots[sl], src)
            P.wait_dma("scalar", ld)
            if i == 0:
                P.wait("scalar", gsig["tern_cols"], gsig["sv_tail"])
            if i >= 2:
                P.wait("scalar", p2["add"][i - 2])
                P.wait_dma("scalar", p2["st"][i - 2])
            a(lambda e, sl=sl, s12=s12, nthr_i=nthr_i: e.activation(
                out=p2_s1[s12], in_=p2_lslots[sl], func=AF.Sign,
                bias=tern_cols[:, nthr_i:nthr_i + 1]))
            sa = a(lambda e, sl=sl, s12=s12, thr_i=thr_i: e.activation(
                out=p2_s2[s12], in_=p2_lslots[sl], func=AF.Sign,
                bias=tern_cols[:, thr_i:thr_i + 1]))
            p2["cons"][i] = sa
            P.wait("vector", sa)
            if i >= 2:
                P.wait_dma("vector", p2["st"][i - 2])
            sv = v(lambda e, s12=s12: e.tensor_add(out=p2_s1[s12], in0=p2_s1[s12],
                                                   in1=p2_s2[s12]))
            p2["add"][i] = sv
            P.wait("gpsimd", sv)
            if is_t1:
                dst = dst.rearrange("c p f -> p c f")
                srcap = p2_s1[s12].rearrange("p (c f) -> p c f", c=8)
            else:
                srcap = p2_s1[s12]
            p2["st"][i] = P.dma_sem("gpsimd", P.s2sem[s12], P.s2ctr[s12], dst, srcap)
            if i == 31:
                gsig["t1_done"] = p2["st"][i]
            p2["i"] = i + 1
        if p2["i"] == len(p2_items):
            gsig["t2_done"] = p2["st"][len(p2_items) - 1]
            gsig["p2_cons_done"] = p2["cons"][len(p2_items) - 1]

    total_sc = SEQ_PER_CORE * NSC_PER_SEQ if n_sc is None else n_sc
    x_slots = (x_sb, arena_f[:, 4096:5120])

    def emit_p1(seq, tok0, act_gate, xbar_gate, xload_gate):
        prev = {}
        h1t_tgt = {}
        for t in range(TT):
            sl = t % 2
            xs_ = x_slots[sl]
            if t < 2:
                if xload_gate is not None:
                    P.wait("gpsimd", xload_gate)
            else:
                P.wait("gpsimd", prev[t - 2])
            s_x = P.dma_sem("gpsimd", P.xlsem[sl], P.xlctr[sl], xs_[:],
                            x_d[seq, tok0 + t * 128: tok0 + (t + 1) * 128, :])
            P.wait_dma("vector", s_x)
            emit_ln_stats(xs_, sl)
            v(lambda e, t=t, sl=sl: e.tensor_copy(out=lnstats[:, t:t + 1],
                                                  in_=rcol[:, 2 * sl:2 * sl + 1]))
            v(lambda e, t=t, sl=sl: e.tensor_copy(out=lnstats[:, TT + t:TT + t + 1],
                                                  in_=rcol[:, 2 * sl + 1:2 * sl + 2]))
            sv_nm = P.sig("vector")
            P.wait("scalar", sv_nm)
            if t == 0 and act_gate is not None:
                for ag in act_gate:
                    P.wait("scalar", ag)
            sv = a(lambda e, t=t, sl=sl, xs_=xs_: e.activation(
                out=h1bf_tm[:, t * 1024:(t + 1) * 1024], in_=xs_[:], func=AF.Identity,
                scale=rcol[:, 2 * sl:2 * sl + 1], bias=rcol[:, 2 * sl + 1:2 * sl + 2]))
            prev[t] = sv
            P.wait("sync", sv)
            if t == 0 and xbar_gate is not None:
                P.wait("sync", xbar_gate)
            h_prev = P.hctr[sl].v
            for kc in range(NKC_DM):
                P.dma_sem("sync", P.hsem[sl], P.hctr[sl],
                          h1_fm[:, kc * 512 + t * 128: kc * 512 + (t + 1) * 128],
                          h1bf_tm[:, t * 1024 + kc * 128: t * 1024 + (kc + 1) * 128],
                          transpose=True, after_sem=(h_prev if kc == 0 else 0))
            h1t_tgt[t] = (P.hsem[sl], P.hctr[sl].v)
        return h1t_tgt

    h1_done = emit_p1(0, 0, None, None, None)

    for sc_i in range(total_sc):
        seq = sc_i // NSC_PER_SEQ
        sci = sc_i % NSC_PER_SEQ
        tok0 = sci * T
        if sci == 0 and sc_i > 0:
            v(lambda e: e.memset(s_all[:], 0.0))
            v(lambda e: e.memset(halo_sb[:], 0.0))
            P.barrier()

        reqs = []
        for gi in range(5):
            ncols = 512 if gi < 4 else DTP
            reqs.append((NKC_DM * ncols, ("k", NKC_DM),
                         scr_wzdt[:, :, gi * 512: gi * 512 + ncols].rearrange("k p f -> p k f")))
        for gx in range(6):
            reqs.append((3 * NKC_DM * 128, ("ck", 3, NKC_DM),
                         scr_wxbc[gx * 3:(gx + 1) * 3].rearrange("c k p f -> p c k f")))
        for c0, ncc in ((0, 8), (8, 8), (16, 2)):
            reqs.append((ncc * 4 * 128, ("ck", ncc, 4),
                         scr_conv[c0:c0 + ncc].rearrange("c k p f -> p c k f")))
        for gx in range(4):
            reqs.append((4 * 1024, ("k", 4),
                         scr_wout[gx * 4:(gx + 1) * 4].rearrange("k p f -> p k f")))
        for gx in range(8):
            reqs.append((4 * NKC_DM * 128, ("ck", 4, NKC_DM),
                         scr_t1[gx * 4:(gx + 1) * 4].rearrange("c k p f -> p c k f")))
        for gx in range(8):
            reqs.append((4 * 1024, ("k", 4),
                         scr_t2[gx * 4:(gx + 1) * 4].rearrange("k p f -> p k f")))

        ring = {"issued": 0, "tgts": {}, "cons": {}}
        if sc_i == 0:
            req_gates = ([cast_sig["zdt"]] * 5
                         + [cast_sig[("xbc", g)] for g in range(6)]
                         + [cast_sig[("conv", g)] for g in range(3)]
                         + [cast_sig["wout"]] * 4
                         + ["t1_done"] * 8 + ["t2_done"] * 8)
        else:
            req_gates = None

        def ring_issue_one(ring=ring, reqs=reqs, req_gates=req_gates):
            kk = ring["issued"]
            if kk >= len(reqs):
                return
            cols, view, src = reqs[kk]
            if req_gates is not None:
                gk = req_gates[kk]
                if isinstance(gk, str):
                    gk = gsig[gk]
                P.wait("scalar", gk)
            if kk >= 2:
                P.wait("scalar", ring["cons"][kk - 2])
            slot = kk % 2
            base = slot * WH
            dst = warena[:, base:base + cols]
            if view[0] == "k":
                dst = dst.rearrange("p (k f) -> p k f", k=view[1])
            else:
                dst = dst.rearrange("p (c k f) -> p c k f", c=view[1], k=view[2])
            ring["tgts"][kk] = P.dma_sem("scalar", P.wsem[slot], P.wctr[slot], dst, src)
            ring["issued"] += 1

        def ring_use(kk, ring=ring):
            P.wait_dma("tensor", ring["tgts"][kk])
            return (kk % 2) * WH

        def ring_done(kk, sig, ring=ring):
            ring["cons"][kk] = sig
            ring_issue_one()

        ring_issue_one()
        ring_issue_one()

        for gi in range(5):
            ncols = 512 if gi < 4 else DTP
            base = ring_use(gi)
            pe_sigs = []
            for t in range(TT):
                bidx = t if gi < 4 else 4 + t
                bank = pb[bidx]
                pb_wait(bidx)
                if gi == 0:
                    P.wait_dma("tensor", h1_done[t])
                for kc in range(NKC_DM):
                    args = (bank[:, 0:ncols],
                            h1_fm[:, kc * 512 + t * 128: kc * 512 + (t + 1) * 128],
                            warena[:, base + kc * ncols: base + (kc + 1) * ncols])
                    if kc < NKC_DM - 1:
                        mm(*args, start=(kc == 0), stop=False)
                    else:
                        pe_sigs.append(mm_inc(*args, start=False, stop=True))
            ring_done(gi, pe_sigs[-1])
            if gi < 4:
                for t in range(TT):
                    P.wait("scalar", pe_sigs[t])
                    sa = a(lambda e, t=t, gi=gi: e.activation(
                        out=sz_tm[:, t * DI + gi * 512: t * DI + (gi + 1) * 512],
                        in_=pb[t][:, 0:512], func=AF.Silu))
                    pb_free(sa, t)
            else:
                for t in range(TT):
                    P.wait("vector", pe_sigs[t])
                    sv = v(lambda e, t=t: e.tensor_add(out=dtu[:], in0=pb[4 + t][:, 0:DTP],
                                                       in1=c_dtb[:]))
                    pb_free(sv, 4 + t)
                    P.wait("scalar", sv)
                    a(lambda e: e.activation(out=dtt[:], in_=dtu[:], func=AF.Abs))
                    a(lambda e: e.activation(out=dtt[:], in_=dtt[:], func=AF.Exp, scale=-1.0))
                    sa = a(lambda e: e.activation(out=dtt[:], in_=dtt[:], func=AF.Ln,
                                                  bias=c_one[:]))
                    P.wait("vector", sa)
                    v(lambda e: e.tensor_relu(out=dtu[:], in_=dtu[:]))
                    v(lambda e, t=t: e.tensor_add(out=dt_tm[:, t * NH:(t + 1) * NH],
                                                  in0=dtt[:], in1=dtu[:]))
            if sc_i == 0:
                drain_pass1(8)

        sv_halo_head = v(lambda e: e.tensor_copy(
            out=xbcraw[:].rearrange("p (c w) -> p c w", c=NCC_XBC)[:, :, 0:3],
            in_=halo_sb[:].rearrange("p (c w) -> p c w", c=NCC_XBC)))
        p4sig = {}
        for gx in range(6):
            base = ring_use(5 + gx)
            last = None
            for i in range(3):
                cc = gx * 3 + i
                b = cc % 2
                pb_wait(b)
                for kc in range(NKC_DM):
                    args = (pb[b][:, 0:512],
                            warena[:, base + (i * NKC_DM + kc) * 128:
                                   base + (i * NKC_DM + kc + 1) * 128],
                            h1_fm[:, kc * 512:(kc + 1) * 512])
                    if kc < NKC_DM - 1:
                        mm(*args, start=(kc == 0), stop=False)
                    else:
                        last = mm_inc(*args, start=False, stop=True)
                P.wait("scalar", last)
                sa = a(lambda e, cc=cc, b=b: e.activation(
                    out=xbcraw[:, cc * (T + 3) + 3: cc * (T + 3) + 3 + T],
                    in_=pb[b][:, 0:512], func=AF.Identity))
                p4sig[cc] = sa
                pb_free(sa, b)
            ring_done(5 + gx, last)
            if sc_i == 0:
                drain_pass1(3)

        for gi2, (c0, ncc) in enumerate(((0, 8), (8, 8), (16, 2))):
            base = ring_use(11 + gi2)
            last = None
            for i in range(ncc):
                cc = c0 + i
                b = 4 + (cc % 2)
                pb_wait(b)
                P.wait("tensor", p4sig[cc])
                if cc == 0:
                    P.wait("tensor", sv_halo_head)
                for k in range(4):
                    args = (pb[b][:, 0:512],
                            warena[:, base + (i * 4 + k) * 128: base + (i * 4 + k + 1) * 128],
                            xbcraw[:, cc * (T + 3) + k: cc * (T + 3) + k + T])
                    if k < 3:
                        mm(*args, start=(k == 0), stop=False)
                    else:
                        last = mm_inc(*args, start=False, stop=True)
                P.wait("scalar", last)
                if cc < NCC_XS:
                    dst = xs_fm[:, cc * T:(cc + 1) * T]
                elif cc == NCC_XS:
                    dst = bfc[:]
                else:
                    dst = cfc[:]
                sa = a(lambda e, dst=dst, b=b, cc=cc: e.activation(
                    out=dst, in_=pb[b][:, 0:512], func=AF.Silu,
                    bias=c_convb[:, cc:cc + 1]))
                pb_free(sa, b)
            ring_done(11 + gi2, last)
            if sc_i == 0:
                drain_pass1(2)
        conv_done = P.sig("scalar")
        P.wait("vector", P.sig("scalar"))
        sv_tail = v(lambda e: e.tensor_copy(
            out=halo_sb[:].rearrange("p (c w) -> p c w", c=NCC_XBC),
            in_=xbcraw[:].rearrange("p (c w) -> p c w", c=NCC_XBC)[:, :, T:T + 3]))
        if sc_i == 0:
            drain_pass1(len(p1_items))
            emit_thr()
            gsig["sv_tail"] = sv_tail

        pe_after_p5 = P.sig("tensor")

        st_tgts = {}
        for qi in range(TT):
            sv_a = v(lambda e, qi=qi: e.tensor_mul(out=dtu[:],
                                                   in0=dt_tm[:, qi * NH:(qi + 1) * NH],
                                                   in1=c_arep[:]))
            P.wait("tensor", sv_a)
            pb_wait(4)
            spe_lt = mm_inc(pb[4][:, 0:NH], c_lt[:], dtu[:], True, True)
            P.wait("vector", spe_lt)
            sv_ac = v(lambda e, qi=qi: e.tensor_copy(out=acum4[:, qi * NH:(qi + 1) * NH],
                                                     in_=pb[4][:, 0:NH]))
            P.wait("tensor", sv_ac)
            spe_tr = tr_inc(pb[4][0:32, 128:256], acum4[:, qi * NH:(qi + 1) * NH], c_ident[:])
            P.wait("scalar", spe_tr)
            if qi > 0:
                P.wait_dma("scalar", st_tgts[qi - 1])
            sa_fm = a(lambda e: e.activation(out=acum_fm[:], in_=pb[4][0:32, 128:256],
                                             func=AF.Identity))
            a(lambda e, qi=qi: e.activation(out=cq4[:, qi * NH:(qi + 1) * NH],
                                            in_=acum4[:, qi * NH:(qi + 1) * NH], func=AF.Exp))
            pb_free(sa_fm, 4)
            st_tgts[qi] = P.dma_sem("scalar", P.rbsem, P.rbctr,
                                    scr_ac[qi].rearrange("(h i) -> h i", h=32), acum_fm[:],
                                    after_own=True)
            pb_wait(5)
            if qi == 0:
                P.wait("tensor", conv_done)
            spe_g = mm_inc(pb[5][:, 0:128], bfc[:, qi * 128:(qi + 1) * 128],
                           cfc[:, qi * 128:(qi + 1) * 128], True, True)
            P.wait("vector", spe_g)
            sv_gm = v(lambda e, qi=qi: e.tensor_mul(out=gm4[:, qi * 128:(qi + 1) * 128],
                                                    in0=pb[5][:, 0:128], in1=c_lt[:]))
            pb_free(sv_gm, 5)
        prep_gm = P.sig("vector")
        prep_cq = P.sig("scalar")

        xsxb_tgt = {}
        xs_free = {}

        def emit_xs_xbars(qi):
            sl = qi % 2
            if qi >= 2:
                P.wait("sync", xs_free[qi - 2])
            for cc in range(NCC_XS):
                P.dma_sem("sync", P.xssem[sl], P.xsctr[sl],
                          xs_tm[:, sl * DI + cc * 128: sl * DI + (cc + 1) * 128],
                          xs_fm[:, cc * T + qi * 128: cc * T + (qi + 1) * 128], transpose=True)
            P.dma_sem("sync", P.xssem[sl], P.xsctr[sl],
                      b_tm[:, sl * DS:(sl + 1) * DS],
                      bfc[:, qi * 128:(qi + 1) * 128], transpose=True)
            xsxb_tgt[qi] = (P.xssem[sl], P.xsctr[sl].v)

        P.wait("sync", conv_done)
        emit_xs_xbars(0)
        emit_xs_xbars(1)

        r_tgt = {}
        r_free = {}

        def emit_r_load(j):
            qi2, half2 = divmod(j, 2)
            sl = j % 2
            if j >= 2:
                P.wait("scalar", r_free[j - 2][0], r_free[j - 2][1])
            if sc_i == 0 and j < 2:
                P.wait("scalar", gsig["p1_done"])
            if sl == 1 and j == 1:
                P.wait("scalar", sv_tail, pe_after_p5)
            P.wait_dma("scalar", st_tgts[qi2])
            r_tgt[j] = P.dma_sem("scalar", P.rsem[sl], P.rctr[sl], r_slot[sl][:],
                                 scr_ac[qi2][half2 * 2048:(half2 + 1) * 2048]
                                 .unsqueeze(0).broadcast_to([128, 2048]))

        emit_r_load(0)
        emit_r_load(1)

        e_sl = (e_bf, e_bf2)
        mt_sl = (mt_bf, mt_bf2)
        xdt_sl = (xdt_bf, xdt_bf2)
        spe_half = {}
        sg_mt_s = {}
        sg_wx_s = {}
        sg_xdt_s = {}
        sa_cl_s = {}
        sbf_cast = {}
        sg_ss_par = {0: None, 1: None}
        sv_sadd_s = {}
        sg_d1_prev = None
        sv_yadd2_prev = None
        ynxb_prev = None

        def emit_echain(j):
            qi2, half2 = divmod(j, 2)
            h0 = half2 * 16
            rb = r_slot[j % 2]
            P.wait_dma("scalar", r_tgt[j])
            if sg_ss_par[half2] is not None:
                P.wait("scalar", sg_ss_par[half2])
            sa_cl = a(lambda e, h0=h0, rb=rb: e.activation(
                out=clast_rep[:, h0:h0 + 16],
                in_=rb[:].rearrange("p (h i) -> p h i", h=16)[:, :, 127], func=AF.Exp))
            sa_cl_s[j] = sa_cl
            P.wait("vector", sa_cl)
            sv_d = v(lambda e, h0=h0, qi2=qi2, rb=rb: e.tensor_sub(
                out=rb[:].rearrange("p (h i) -> p h i", h=16),
                in0=rb[:].rearrange("p (h i) -> p h i", h=16),
                in1=acum4[:, qi2 * NH + h0: qi2 * NH + h0 + 16]
                .unsqueeze(2).broadcast_to([128, 16, 128])))
            P.wait("scalar", sv_d)
            if sg_wx_s.get(j - 2) is not None:
                P.wait("scalar", sg_wx_s[j - 2])
            a(lambda e, h0=h0, rb=rb: e.activation(
                out=wdec[:, h0:h0 + 16],
                in_=rb[:].rearrange("p (h i) -> p h i", h=16)[:, :, 127], func=AF.Exp))
            if sg_mt_s.get(j - 2) is not None:
                P.wait("scalar", sg_mt_s[j - 2])
            esl = e_sl[j % 2]
            sa_e = a(lambda e, rb=rb, esl=esl: e.activation(out=esl[:], in_=rb[:],
                                                            func=AF.Exp))
            P.wait("vector", sa_e)
            sv_e = v(lambda e, esl=esl: e.tensor_scalar_min(out=esl[:], in0=esl[:],
                                                            scalar1=1e30))
            r_free[j] = (sa_e, sv_d)
            if j + 2 < 2 * TT:
                emit_r_load(j + 2)
            return sv_e

        def emit_mtwx(j, sv_e):
            qi2, half2 = divmod(j, 2)
            h0 = half2 * 16
            P.wait("gpsimd", sv_e)
            if j == 0:
                P.wait("gpsimd", prep_gm)
            if spe_half.get(j - 2) is not None:
                P.wait("gpsimd", spe_half[j - 2])
            esl = e_sl[j % 2]
            msl = mt_sl[j % 2]
            xsl = xdt_sl[qi2 % 2]
            sg_mt = g(lambda e, qi2=qi2, esl=esl, msl=msl: e.tensor_tensor(
                out=msl[:].rearrange("p (h i) -> p h i", h=16),
                in0=esl[:].rearrange("p (h i) -> p h i", h=16),
                in1=gm4[:, qi2 * 128:(qi2 + 1) * 128].unsqueeze(1)
                .broadcast_to([128, 16, 128]),
                op=ALU.mult))
            sg_wx = g(lambda e, h0=h0, xsl=xsl: e.tensor_tensor(
                out=wxdt_bf[:, h0 * HD:(h0 + 16) * HD].rearrange("p (h w) -> p h w", h=16),
                in0=xsl[:, h0 * HD:(h0 + 16) * HD].rearrange("p (h w) -> p h w", h=16),
                in1=wdec[:, h0:h0 + 16].unsqueeze(2).broadcast_to([128, 16, HD]),
                op=ALU.mult))
            sg_mt_s[j] = sg_mt
            sg_wx_s[j] = sg_wx

        def emit_xdt(qi):
            P.wait_dma("gpsimd", xsxb_tgt[qi])
            if spe_half.get(2 * qi - 3) is not None:
                P.wait("gpsimd", spe_half[2 * qi - 3])
            sl_q = qi % 2
            xsl = xdt_sl[sl_q]
            sg_xdt = g(lambda e, qi=qi, sl_q=sl_q, xsl=xsl: e.tensor_tensor(
                out=xsl[:].rearrange("p (h w) -> p h w", h=NH),
                in0=xs_tm[:, sl_q * DI:(sl_q + 1) * DI].rearrange("p (h w) -> p h w", h=NH),
                in1=dt_tm[:, qi * NH:(qi + 1) * NH].unsqueeze(2).broadcast_to([128, NH, HD]),
                op=ALU.mult))
            sg_xdt_s[qi] = sg_xdt

        def emit_sbf_cast(qi, half):
            h0 = half * 16
            sv_c = v(lambda e, h0=h0: e.tensor_copy(
                out=s_bf[:, h0 * HD:(h0 + 16) * HD], in_=s_all[:, h0 * HD:(h0 + 16) * HD]))
            sbf_cast[(qi, half)] = sv_c

        emit_xdt(0)
        emit_mtwx(0, emit_echain(0))
        emit_sbf_cast(0, 0)
        emit_sbf_cast(0, 1)

        for qi in range(TT):
            sl_q = qi % 2
            sv_y1_last = None
            for half in range(2):
                j = 2 * qi + half
                h0 = half * 16
                if j + 1 < 2 * TT:
                    qn, hn = divmod(j + 1, 2)
                    if hn == 0:
                        emit_xdt(qn)
                    emit_mtwx(j + 1, emit_echain(j + 1))
                if sc_i == 0:
                    drain_pass2(8)
                Yb = (0, 1) if half == 0 else (6, 7)
                Vb = (2, 3) if half == 0 else (0, 1)
                Sb = (4, 5) if half == 0 else (2, 3)
                Ypw = pbw[Yb[0] // 2]
                Vpw = pbw[Vb[0] // 2]
                Spw = pbw[Sb[0] // 2]
                msl = mt_sl[j % 2]
                xsl = xdt_sl[sl_q]
                P.wait("tensor", sg_mt_s[j])
                if half == 0:
                    P.wait("tensor", sg_xdt_s[qi])
                pb_wait(*Yb)
                for hl in range(16):
                    h = h0 + hl
                    bank = pb[Yb[hl // 8]]
                    mm(bank[:, (hl % 8) * 64:(hl % 8 + 1) * 64],
                       msl[:, hl * 128:(hl + 1) * 128],
                       xsl[:, h * HD:(h + 1) * HD],
                       start=(hl % 8 == 0), stop=(hl % 8 == 7))
                pb_wait(*Vb)
                P.wait("tensor", sbf_cast[(qi, half)])
                for b2 in range(2):
                    mm(pb[Vb[b2]][:, 0:512],
                       cfc[:, qi * 128:(qi + 1) * 128],
                       s_bf[:, (h0 + b2 * 8) * HD:(h0 + (b2 + 1) * 8) * HD],
                       start=True, stop=True)
                pb_wait(*Sb)
                if half == 0:
                    P.wait_dma("tensor", xsxb_tgt[qi])
                P.wait("tensor", sg_wx_s[j])
                spe_all = None
                for b2 in range(2):
                    args = (pb[Sb[b2]][:, 0:512],
                            b_tm[:, sl_q * DS:(sl_q + 1) * DS],
                            wxdt_bf[:, (h0 + b2 * 8) * HD:(h0 + (b2 + 1) * 8) * HD])
                    if b2 == 0:
                        mm(*args, start=True, stop=True)
                    else:
                        spe_all = mm_inc(*args, start=True, stop=True)
                spe_half[j] = spe_all
                P.wait("vector", spe_all)
                if j == 0:
                    P.wait("vector", prep_cq)
                if half == 0 and sg_d1_prev is not None:
                    P.wait("vector", sg_d1_prev)
                sv_vc = v(lambda e, qi=qi, h0=h0, Vpw=Vpw: e.tensor_mul(
                    out=vc_sb[:].rearrange("p (h w) -> p h w", h=16),
                    in0=Vpw[:].rearrange("p (h w) -> p h w", h=16),
                    in1=cq4[:, qi * NH + h0: qi * NH + h0 + 16]
                    .unsqueeze(2).broadcast_to([128, 16, HD])))
                pb_free(sv_vc, *Vb)
                sv_y1 = v(lambda e, half=half, Ypw=Ypw: e.tensor_add(
                    out=y_sb[:, half * 1024:(half + 1) * 1024], in0=Ypw[:], in1=vc_sb[:]))
                pb_free(sv_y1, *Yb)
                P.wait("gpsimd", sa_cl_s[j])
                if sv_sadd_s.get(j - 1) is not None:
                    P.wait("gpsimd", sv_sadd_s[j - 1])
                if half == 0 and sv_yadd2_prev is not None:
                    P.wait("gpsimd", sv_yadd2_prev)
                sg_ss = g(lambda e, h0=h0: e.tensor_tensor(
                    out=stmp[:].rearrange("p (h w) -> p h w", h=16),
                    in0=s_all[:, h0 * HD:(h0 + 16) * HD].rearrange("p (h w) -> p h w", h=16),
                    in1=clast_rep[:, h0:h0 + 16].unsqueeze(2).broadcast_to([128, 16, HD]),
                    op=ALU.mult))
                sg_ss_par[half] = sg_ss
                P.wait("vector", sg_ss)
                sv_sadd = v(lambda e, h0=h0, Spw=Spw: e.tensor_add(
                    out=s_all[:, h0 * HD:(h0 + 16) * HD], in0=stmp[:], in1=Spw[:]))
                pb_free(sv_sadd, *Sb)
                sv_sadd_s[j] = sv_sadd
                if qi + 1 < TT:
                    emit_sbf_cast(qi + 1, half)
                sv_y1_last = sv_y1

            P.wait("gpsimd", sv_y1_last, sv_sadd_s[2 * qi + 1])
            sg_d1 = g(lambda e, sl_q=sl_q: e.tensor_tensor(
                out=vc_sb[:].rearrange("p (h w) -> p h w", h=16),
                in0=xs_tm[:, sl_q * DI: sl_q * DI + 1024].rearrange("p (h w) -> p h w", h=16),
                in1=c_drep[:, 0:16].unsqueeze(2).broadcast_to([128, 16, HD]), op=ALU.mult))
            sg_d1_prev = sg_d1
            sg_d2 = g(lambda e, sl_q=sl_q: e.tensor_tensor(
                out=stmp[:].rearrange("p (h w) -> p h w", h=16),
                in0=xs_tm[:, sl_q * DI + 1024: sl_q * DI + 2048]
                .rearrange("p (h w) -> p h w", h=16),
                in1=c_drep[:, 16:32].unsqueeze(2).broadcast_to([128, 16, HD]), op=ALU.mult))
            xs_free[qi] = sg_d2
            if qi + 2 < TT:
                emit_xs_xbars(qi + 2)
            P.wait("vector", sg_d2)
            v(lambda e: e.tensor_add(out=y_sb[:, 0:1024], in0=y_sb[:, 0:1024], in1=vc_sb[:]))
            sv_yadd2 = v(lambda e: e.tensor_add(out=y_sb[:, 1024:2048],
                                                in0=y_sb[:, 1024:2048], in1=stmp[:]))
            sv_yadd2_prev = sv_yadd2
            sv_gate = v(lambda e, qi=qi: e.tensor_mul(out=y_sb[:], in0=y_sb[:],
                                                      in1=sz_tm[:, qi * DI:(qi + 1) * DI]))
            sv_sq = v(lambda e: e.scalar_tensor_tensor(
                out=sq_scr[:], in0=y_sb[:], scalar=1.0, in1=y_sb[:],
                op0=ALU.mult, op1=ALU.mult, accum_out=msq[:]))
            P.wait("scalar", sv_sq)
            a(lambda e: e.activation(out=msq[:], in_=msq[:], func=AF.Ln, bias=c_eps[:],
                                     scale=1.0 / DI))
            sa_r = a(lambda e: e.activation(out=msq[:], in_=msq[:], func=AF.Exp, scale=-0.5))
            P.wait("vector", sa_r)
            if ynxb_prev is not None:
                for yp in ynxb_prev:
                    P.wait_dma("vector", yp)
            sv_yn = v(lambda e: e.scalar_tensor_tensor(
                out=ynbf_tm[:], in0=y_sb[:], scalar=msq[:], in1=c_normw[:],
                op0=ALU.mult, op1=ALU.mult))
            P.wait("sync", sv_yn)
            for cc in range(NCC_XS):
                gidx = cc // 4
                P.dma_sem("sync", P.ynsem[gidx], P.ynctr[gidx],
                          ynfm[:, cc * T + qi * 128: cc * T + (qi + 1) * 128],
                          ynbf_tm[:, cc * 128:(cc + 1) * 128], transpose=True)
            ynxb_prev = tuple((P.ynsem[i], P.ynctr[i].v) for i in range(4))

        if sc_i == 0:
            drain_pass2(len(p2_items))
        yn_tgts = tuple((P.ynsem[i], P.ynctr[i].v) for i in range(4))
        p8sig = {}
        for gx in range(4):
            base = ring_use(14 + gx)
            P.wait_dma("tensor", yn_tgts[gx])
            if gx == 0:
                pb_wait(*range(8))
            last = None
            for t in range(TT):
                for i in range(4):
                    kc = gx * 4 + i
                    for b2 in range(2):
                        args = (pb[2 * t + b2][:, 0:512],
                                ynfm[:, kc * T + t * 128: kc * T + (t + 1) * 128],
                                warena[:, base + i * 1024 + b2 * 512:
                                       base + i * 1024 + (b2 + 1) * 512])
                        start = (gx == 0 and i == 0)
                        stop = (gx == 3 and i == 3)
                        group_last = (t == TT - 1 and i == 3 and b2 == 1)
                        if stop or group_last:
                            s = mm_inc(*args, start=start, stop=stop)
                            if stop:
                                p8sig[(t, b2)] = s
                            last = s
                        else:
                            mm(*args, start=start, stop=False)
            ring_done(14 + gx, last)
        h2t = {}
        at_slots = (at_sb, f_sb)
        h1t_slots = (h1t_sb, o_sb)
        sgx2 = {}
        for t in range(TT):
            sl = t % 2
            rsl = 2 + sl
            xs_ = x_slots[sl]
            if t >= 2:
                P.wait("gpsimd", sgx2[t - 2])
            s_x = P.dma_sem("gpsimd", P.xlsem[sl], P.xlctr[sl], xs_[:],
                            x_d[seq, tok0 + t * 128: tok0 + (t + 1) * 128, :])
            P.wait_dma("scalar", s_x)
            sa_h1t = a(lambda e, t=t, xs_=xs_, sl=sl: e.activation(
                out=h1t_slots[sl][:], in_=xs_[:], func=AF.Identity,
                scale=lnstats[:, t:t + 1], bias=lnstats[:, TT + t:TT + t + 1]))
            P.wait("vector", p8sig[(t, 0)], p8sig[(t, 1)], sa_h1t)
            sv_at = v(lambda e, t=t, sl=sl: e.tensor_add(
                out=at_slots[sl][:], in0=h1t_slots[sl][:], in1=pbw[t][:]))
            pb_free(sv_at, 2 * t, 2 * t + 1)
            emit_ln_stats(at_slots[sl], rsl)
            sv_nm = P.sig("vector")
            P.wait("scalar", sv_nm)
            sa_ln = a(lambda e, sl=sl, rsl=rsl: e.activation(
                out=at_slots[sl][:], in_=at_slots[sl][:], func=AF.Identity,
                scale=rcol[:, 2 * rsl:2 * rsl + 1], bias=rcol[:, 2 * rsl + 1:2 * rsl + 2]))
            P.wait("gpsimd", sa_ln)
            if sc_i == 0 and t == 0:
                P.wait("gpsimd", gsig["p2_cons_done"])
            sg_x2 = g(lambda e, t=t, sl=sl, xs_=xs_: e.tensor_add(
                out=x2_sb[:, t * 1024:(t + 1) * 1024], in0=xs_[:], in1=at_slots[sl][:]))
            sgx2[t] = sg_x2
            P.wait("vector", sg_x2)
            emit_ln_stats(x2_sb[:, t * 1024:(t + 1) * 1024], rsl)
            sv_nm2a = P.sig("vector")
            P.wait("scalar", sv_nm2a)
            sv_nm2 = a(lambda e, t=t, rsl=rsl: e.activation(
                out=h2bf_tm[:, t * 1024:(t + 1) * 1024],
                in_=x2_sb[:, t * 1024:(t + 1) * 1024], func=AF.Identity,
                scale=rcol[:, 2 * rsl:2 * rsl + 1], bias=rcol[:, 2 * rsl + 1:2 * rsl + 2]))
            P.wait("sync", sv_nm2)
            h_prev = P.hctr[sl].v
            for kc in range(NKC_DM):
                P.dma_sem("sync", P.hsem[sl], P.hctr[sl],
                          h2_fm[:, kc * 512 + t * 128: kc * 512 + (t + 1) * 128],
                          h2bf_tm[:, t * 1024 + kc * 128: t * 1024 + (kc + 1) * 128],
                          transpose=True, after_sem=(h_prev if kc == 0 else 0))
            h2t[t] = (P.hsem[sl], P.hctr[sl].v)
        p8_act_done = P.sig("scalar")
        p8_pool_done = P.sig("gpsimd")
        P.wait("scalar", sv_gate)
        for yp in yn_tgts:
            P.wait_dma("scalar", yp)
        for gx in range(8):
            base = ring_use(18 + gx)
            last = None
            for i in range(4):
                cc = gx * 4 + i
                b = cc % 2
                pb_wait(b)
                for t2 in range(TT):
                    if gx == 0 and i == 0:
                        P.wait_dma("tensor", h2t[t2])
                    for kc in range(NKC_DM):
                        args = (pb[b][:, t2 * 128:(t2 + 1) * 128],
                                warena[:, base + (i * NKC_DM + kc) * 128:
                                       base + (i * NKC_DM + kc + 1) * 128],
                                h2_fm[:, kc * 512 + t2 * 128: kc * 512 + (t2 + 1) * 128])
                        if kc < NKC_DM - 1 or t2 < TT - 1:
                            mm(*args, start=(kc == 0), stop=(kc == NKC_DM - 1))
                        else:
                            last = mm_inc(*args, start=False, stop=True)
                P.wait("scalar", last)
                sa = a(lambda e, cc=cc, b=b: e.activation(
                    out=hg_fm[:, cc * T:(cc + 1) * T], in_=pb[b][:, 0:512],
                    func=AF.Gelu, bias=c_b1[:, cc:cc + 1], scale=0.5))
                pb_free(sa, b)
            ring_done(18 + gx, last)
            p9_pe_done = last

        p10sig = {}
        for gx in range(8):
            base = ring_use(26 + gx)
            if gx == 0:
                pb_wait(*range(8))
            last = None
            for t in range(TT):
                for i in range(4):
                    kc = gx * 4 + i
                    for b2 in range(2):
                        args = (pb[2 * t + b2][:, 0:512],
                                hg_fm[:, kc * T + t * 128: kc * T + (t + 1) * 128],
                                warena[:, base + i * 1024 + b2 * 512:
                                       base + i * 1024 + (b2 + 1) * 512])
                        start = (gx == 0 and i == 0)
                        stop = (gx == 7 and i == 3)
                        group_last = (t == TT - 1 and i == 3 and b2 == 1)
                        if stop or group_last:
                            s = mm_inc(*args, start=start, stop=stop)
                            if stop:
                                p10sig[t] = s
                            last = s
                        else:
                            mm(*args, start=start, stop=False)
            ring_done(26 + gx, last)
        if sc_i + 1 < total_sc:
            nseq = (sc_i + 1) // NSC_PER_SEQ
            ntok0 = ((sc_i + 1) % NSC_PER_SEQ) * T
            h1_done = emit_p1(nseq, ntok0, (h2t[TT - 1],), p9_pe_done, p8_pool_done)
        o_slots = (o_sb, stmp)
        f_slots = (f_sb, at_sb)
        sgo = {}
        for t in range(TT):
            sl = t % 2
            rsl = 2 + sl
            P.wait("vector", p10sig[t])
            if t >= 2:
                P.wait("vector", sgo[t - 2])
            sv_f = v(lambda e, t=t, sl=sl: e.scalar_tensor_tensor(
                out=f_slots[sl][:], in0=pbw[t][:], scalar=0.5, in1=c_b2[:],
                op0=ALU.mult, op1=ALU.add))
            pb_free(sv_f, 2 * t, 2 * t + 1)
            emit_ln_stats(f_slots[sl], rsl)
            sv_nm = P.sig("vector")
            P.wait("scalar", sv_nm)
            sa_ln = a(lambda e, sl=sl, rsl=rsl: e.activation(
                out=f_slots[sl][:], in_=f_slots[sl][:], func=AF.Identity,
                scale=rcol[:, 2 * rsl:2 * rsl + 1], bias=rcol[:, 2 * rsl + 1:2 * rsl + 2]))
            slot = o_slots[sl]
            P.wait("gpsimd", sa_ln)
            if t >= 2:
                tprev = P.stctr.v
                P.on("gpsimd", lambda e, tprev=tprev: e.wait_ge(P.stsem, tprev))
            sg_o = g(lambda e, t=t, slot=slot, sl=sl: e.tensor_add(
                out=slot[:], in0=x2_sb[:, t * 1024:(t + 1) * 1024], in1=f_slots[sl][:]))
            P.wait("gpsimd", sg_o)
            sgo[t] = sg_o
            P.store_out("gpsimd", out_d[seq, tok0 + t * 128: tok0 + (t + 1) * 128, :], slot[:])
        P.barrier()

    st_total = P.stctr.v

    def fin(e):
        e.wait_ge(P.stsem, st_total)
    P.on("gpsimd", fin)

    P.finalize_block()
    if dbg:
        build_program.taps = _dbg_state["taps"]
    return nc


def _host_prep(inputs):
    f32 = np.float32
    ip = np.ascontiguousarray(inputs["L0_attn_in_proj_w"], dtype=f32)
    wz = ip[0:DI]
    wxbc = ip[DI:DI + CD]
    wdt = ip[DI + CD:]
    wzdt_t = np.ascontiguousarray(np.concatenate([wz, wdt], axis=0).T)
    wxbc_t = np.ascontiguousarray(wxbc.T)
    conv_w = np.ascontiguousarray(inputs["L0_attn_conv_w"], dtype=f32)
    convdiag = np.zeros((NCC_XBC, 128, 4, 128), dtype=f32)
    for cc in range(NCC_XBC):
        for i in range(128):
            convdiag[cc, i, :, i] = conv_w[cc * 128 + i, :]
    convdiag = convdiag.reshape(NCC_XBC, 128, 4 * 128)
    convb_cols = np.ascontiguousarray(
        inputs["L0_attn_conv_b"].astype(f32).reshape(NCC_XBC, 128).T)
    rep = lambda vec, w: np.ascontiguousarray(
        np.broadcast_to(np.asarray(vec, f32).reshape(1, -1), (128, w)))
    dtb_rep = rep(inputs["L0_attn_dt_bias"], NH)
    alog_rep = rep(inputs["L0_attn_A_log"], NH)
    d_rep = rep(inputs["L0_attn_D"], NH)
    normw_rep = rep(inputs["L0_attn_norm_w"], DI)
    b1_cols = np.ascontiguousarray(inputs["L0_ffn_b1"].astype(f32).reshape(NCC_FF, 128).T)
    b2_rep = rep(inputs["L0_ffn_b2"], DM)
    wout_t = np.ascontiguousarray(inputs["L0_attn_out_proj_w"].astype(f32).T)
    w1_t = np.ascontiguousarray(inputs["L0_ffn_W1"].astype(f32).T)
    w2_t = np.ascontiguousarray(inputs["L0_ffn_W2"].astype(f32).T)
    ident = np.eye(128, dtype=f32)
    identb = np.eye(128, dtype=ml_dtypes.bfloat16)
    lt = np.triu(np.ones((128, 128), dtype=f32))
    return {
        "wzdt_t": wzdt_t, "wxbc_t": wxbc_t, "convdiag": convdiag,
        "convb_cols": convb_cols, "dtb_rep": dtb_rep, "alog_rep": alog_rep,
        "d_rep": d_rep, "normw_rep": normw_rep, "b1_cols": b1_cols,
        "b2_rep": b2_rep, "wout_t": wout_t, "w1_t": w1_t, "w2_t": w2_t,
        "ident128": ident, "ident128b": identb, "lt128": lt,
    }


def kernel(**inputs):
    x = np.ascontiguousarray(inputs["L0_attn_x"], dtype=np.float32)
    common = _host_prep(inputs)
    nc = build_program()
    in_maps = []
    for c in range(NCORES):
        m = dict(common)
        m["x_in"] = np.ascontiguousarray(x[c * SEQ_PER_CORE:(c + 1) * SEQ_PER_CORE])
        in_maps.append(m)
    res = run_bass_kernel_spmd(nc, in_maps, core_ids=list(range(NCORES)))
    out = np.concatenate([np.asarray(r["out"]) for r in res.results], axis=0)
    return out.astype(np.float32)


# version 21
# speedup vs baseline: 1.0416x; 1.0416x over previous
import numpy as np
import ml_dtypes
import concourse.bass as bass
import concourse.mybir as mybir
from concourse.bass_utils import run_bass_kernel_spmd

F32 = mybir.dt.float32
BF16 = mybir.dt.bfloat16
AF = mybir.ActivationFunctionType
ALU = mybir.AluOpType

NCORES = 8
SEQ_PER_CORE = 2
L = 2048
DM = 1024
DI = 2048
DS = 128
NH = 32
HD = 64
CD = 2304
DTP = 32
ZDT = DI + DTP
FF = 4096
T = 512
TT = 4
NSC_PER_SEQ = L // T
Q = 128
NCC_XBC = CD // 128
NCC_XS = DI // 128
NKC_DM = DM // 128
NKC_DI = DI // 128
NCC_FF = FF // 128
WH = 4608

ENGS = ("sync", "tensor", "vector", "scalar", "gpsimd")


class Ctr:
    __slots__ = ("v",)

    def __init__(self):
        self.v = 0

    def take(self, n=1):
        self.v += n
        return self.v


class Prog:

    def __init__(self, nc):
        self.nc = nc
        self.q = {e: [] for e in ENGS}
        self.bar = nc.alloc_semaphore("bar")
        self.nbar = 0
        self.esem = {e: nc.alloc_semaphore(f"sig_{e}") for e in ENGS}
        self.opctr = {e: Ctr() for e in ENGS}
        self.dsem = {e: nc.alloc_semaphore(f"dma_{e}") for e in ("sync", "scalar", "gpsimd")}
        self.dctr = {e: Ctr() for e in ("sync", "scalar", "gpsimd")}
        self.stsem = nc.alloc_semaphore("st")
        self.stctr = Ctr()
        self.wsem = [nc.alloc_semaphore("w0"), nc.alloc_semaphore("w1")]
        self.wctr = [Ctr(), Ctr()]
        self.rbsem = nc.alloc_semaphore("rb")
        self.rbctr = Ctr()
        self.rsem = [nc.alloc_semaphore("r0"), nc.alloc_semaphore("r1")]
        self.rctr = [Ctr(), Ctr()]
        self.xssem = [nc.alloc_semaphore("xs0"), nc.alloc_semaphore("xs1")]
        self.xsctr = [Ctr(), Ctr()]
        self.ynsem = [nc.alloc_semaphore(f"yn{i}") for i in range(4)]
        self.ynctr = [Ctr() for _ in range(4)]
        self.hsem = [nc.alloc_semaphore("hx0"), nc.alloc_semaphore("hx1")]
        self.hctr = [Ctr(), Ctr()]
        self.lsem = [nc.alloc_semaphore("l0"), nc.alloc_semaphore("l1")]
        self.lctr = [Ctr(), Ctr()]
        self.xlsem = [nc.alloc_semaphore("xl0"), nc.alloc_semaphore("xl1")]
        self.xlctr = [Ctr(), Ctr()]
        self.s2sem = [nc.alloc_semaphore("s0"), nc.alloc_semaphore("s1")]
        self.s2ctr = [Ctr(), Ctr()]

    def on(self, eng, fn):
        self.q[eng].append(fn)

    def dp(self, eng, fn):
        prev = self.opctr[eng].v
        tgt = self.opctr[eng].take(1)
        sem = self.esem[eng]

        def f(e, fn=fn, prev=prev, sem=sem):
            if prev > 0:
                e.wait_ge(sem, prev)
            fn(e).then_inc(sem, 1)
        self.on(eng, f)
        return (sem, tgt)

    def dma(self, qeng, out, in_, transpose=False, after_own=False):
        tgt = self.dctr[qeng].take(16)
        sem = self.dsem[qeng]
        own = self.opctr[qeng].v if after_own else 0
        osem = self.esem.get(qeng)

        def f(e, out=out, in_=in_, transpose=transpose, sem=sem, own=own, osem=osem):
            if own > 0:
                e.wait_ge(osem, own)
            e.dma_start(out=out, in_=in_, transpose=transpose).then_inc(sem, 16)
        self.on(qeng, f)
        return (self.dsem[qeng], tgt)

    def dma_sem(self, qeng, sem, ctr, out, in_, after_own=False, transpose=False,
                after_sem=0):
        tgt = ctr.take(16)
        own = self.opctr[qeng].v if after_own else 0
        osem = self.esem.get(qeng)

        def f(e, out=out, in_=in_, sem=sem, own=own, osem=osem, transpose=transpose,
              after_sem=after_sem):
            if own > 0:
                e.wait_ge(osem, own)
            if after_sem > 0:
                e.wait_ge(sem, after_sem)
            e.dma_start(out=out, in_=in_, transpose=transpose).then_inc(sem, 16)
        self.on(qeng, f)
        return (sem, tgt)

    def store_out(self, qeng, out, in_):
        tgt = self.stctr.take(16)

        def f(e, out=out, in_=in_):
            e.dma_start(out=out, in_=in_).then_inc(self.stsem, 16)
        self.on(qeng, f)
        return tgt

    def sig(self, eng):
        return (self.esem[eng], self.opctr[eng].v)

    def wait(self, eng, *sigs):
        for s in sigs:
            if s is None:
                continue
            sem, tgt = s
            if tgt <= 0:
                continue

            def f(e, sem=sem, tgt=tgt):
                e.wait_ge(sem, tgt)
            self.on(eng, f)

    def wait_dma(self, eng, sig_):
        self.wait(eng, sig_)

    def barrier(self):
        self.nbar += 1
        k = self.nbar
        dtargets = {e: self.dctr[e].v for e in self.dctr}
        otargets = {e: self.opctr[e].v for e in ENGS}
        st_t = self.stctr.v
        w_t = (self.wctr[0].v, self.wctr[1].v)
        rb_t = self.rbctr.v
        r_t = (self.rctr[0].v, self.rctr[1].v)
        xs_t = (self.xsctr[0].v, self.xsctr[1].v)
        yn_t = tuple(c.v for c in self.ynctr)
        h_t = (self.hctr[0].v, self.hctr[1].v)
        l_t = (self.lctr[0].v, self.lctr[1].v)
        s2_t = (self.s2ctr[0].v, self.s2ctr[1].v)
        xl_t = (self.xlctr[0].v, self.xlctr[1].v)
        for eng in ENGS:
            def f(e, eng=eng, k=k, dtargets=dtargets, otargets=otargets,
                  st_t=st_t, w_t=w_t, rb_t=rb_t, r_t=r_t, xs_t=xs_t, yn_t=yn_t, h_t=h_t,
                  l_t=l_t, s2_t=s2_t, xl_t=xl_t):
                if eng in dtargets and dtargets[eng] > 0:
                    e.wait_ge(self.dsem[eng], dtargets[eng])
                if eng == "gpsimd" and st_t > 0:
                    e.wait_ge(self.stsem, st_t)
                if eng == "scalar":
                    if w_t[0] > 0:
                        e.wait_ge(self.wsem[0], w_t[0])
                    if w_t[1] > 0:
                        e.wait_ge(self.wsem[1], w_t[1])
                    if rb_t > 0:
                        e.wait_ge(self.rbsem, rb_t)
                    for si in (0, 1):
                        if r_t[si] > 0:
                            e.wait_ge(self.rsem[si], r_t[si])
                if eng == "sync":
                    for si in (0, 1):
                        if xs_t[si] > 0:
                            e.wait_ge(self.xssem[si], xs_t[si])
                        if s2_t[si] > 0:
                            e.wait_ge(self.s2sem[si], s2_t[si])
                    for si in (0, 1):
                        if h_t[si] > 0:
                            e.wait_ge(self.hsem[si], h_t[si])
                    for si in range(4):
                        if yn_t[si] > 0:
                            e.wait_ge(self.ynsem[si], yn_t[si])
                if eng == "gpsimd":
                    for si in (0, 1):
                        if l_t[si] > 0:
                            e.wait_ge(self.lsem[si], l_t[si])
                        if xl_t[si] > 0:
                            e.wait_ge(self.xlsem[si], xl_t[si])
                if otargets[eng] > 0:
                    e.wait_ge(self.esem[eng], otargets[eng])
                e.drain()
                e.sem_inc(self.bar, 1)
                e.wait_ge(self.bar, 5 * k)
            self.on(eng, f)

    def finalize_block(self):
        nc = self.nc
        with nc.Block() as block:
            @block.sync
            def _(e):
                for f in self.q["sync"]:
                    f(e)

            @block.tensor
            def _(e):
                for f in self.q["tensor"]:
                    f(e)

            @block.vector
            def _(e):
                for f in self.q["vector"]:
                    f(e)

            @block.scalar
            def _(e):
                for f in self.q["scalar"]:
                    f(e)

            @block.gpsimd
            def _(e):
                for f in self.q["gpsimd"]:
                    f(e)


def build_program(n_sc=None, dbg=False):
    nc = bass.Bass("TRN2", target_bir_lowering=False)
    P = Prog(nc)

    x_d = nc.dram_tensor("x_in", [SEQ_PER_CORE, L, DM], F32, kind="ExternalInput").ap()
    wzdt_d = nc.dram_tensor("wzdt_t", [DM, ZDT], F32, kind="ExternalInput").ap()
    wxbc_d = nc.dram_tensor("wxbc_t", [DM, CD], F32, kind="ExternalInput").ap()
    convdiag_d = nc.dram_tensor("convdiag", [NCC_XBC, 128, 4 * 128], F32, kind="ExternalInput").ap()
    convb_d = nc.dram_tensor("convb_cols", [128, NCC_XBC], F32, kind="ExternalInput").ap()
    dtb_d = nc.dram_tensor("dtb_rep", [128, NH], F32, kind="ExternalInput").ap()
    alog_d = nc.dram_tensor("alog_rep", [128, NH], F32, kind="ExternalInput").ap()
    drep_d = nc.dram_tensor("d_rep", [128, NH], F32, kind="ExternalInput").ap()
    normw_d = nc.dram_tensor("normw_rep", [128, DI], F32, kind="ExternalInput").ap()
    b1_d = nc.dram_tensor("b1_cols", [128, NCC_FF], F32, kind="ExternalInput").ap()
    b2_d = nc.dram_tensor("b2_rep", [128, DM], F32, kind="ExternalInput").ap()
    wout_d = nc.dram_tensor("wout_t", [DI, DM], F32, kind="ExternalInput").ap()
    w1_d = nc.dram_tensor("w1_t", [DM, FF], F32, kind="ExternalInput").ap()
    w2_d = nc.dram_tensor("w2_t", [FF, DM], F32, kind="ExternalInput").ap()
    ident_d = nc.dram_tensor("ident128", [128, 128], F32, kind="ExternalInput").ap()
    identb_d = nc.dram_tensor("ident128b", [128, 128], BF16, kind="ExternalInput").ap()
    lt_d = nc.dram_tensor("lt128", [128, 128], F32, kind="ExternalInput").ap()

    out_d = nc.dram_tensor("out", [SEQ_PER_CORE, L, DM], F32, kind="ExternalOutput").ap()
    if dbg:
        dbgf_d = nc.dram_tensor("dbgf", [128, 16384], F32, kind="ExternalOutput").ap()
        dbgb_d = nc.dram_tensor("dbgb", [128, 32768], BF16, kind="ExternalOutput").ap()
        _dbg_state = {"f": 0, "b": 0, "taps": {}}

    def tap(name, src_ap, kind="f"):
        if not dbg:
            return
        w = src_ap.shape[-1] if len(src_ap.shape) == 2 else int(np.prod(src_ap.shape[1:]))
        st = _dbg_state[kind]
        dst = (dbgf_d if kind == "f" else dbgb_d)[0:src_ap.shape[0], st:st + w]
        _dbg_state["taps"][name] = (kind, st, w, src_ap.shape[0])
        _dbg_state[kind] = st + w
        P.barrier()
        P.dma("gpsimd", dst, src_ap)
        P.barrier()

    build_program.taps = None

    scr_wzdt = nc.dram_tensor("scr_wzdt", [NKC_DM, 128, ZDT], BF16, kind="Internal").ap()
    scr_wxbc = nc.dram_tensor("scr_wxbc", [NCC_XBC, NKC_DM, 128, 128], BF16, kind="Internal").ap()
    scr_conv = nc.dram_tensor("scr_conv", [NCC_XBC, 4, 128, 128], BF16, kind="Internal").ap()
    scr_wout = nc.dram_tensor("scr_wout", [NKC_DI, 128, DM], BF16, kind="Internal").ap()
    scr_t1 = nc.dram_tensor("scr_t1", [NCC_FF, NKC_DM, 128, 128], BF16, kind="Internal").ap()
    scr_t2 = nc.dram_tensor("scr_t2", [NCC_FF, 128, DM], BF16, kind="Internal").ap()
    scr_ac = nc.dram_tensor("scr_ac", [TT, NH * Q], F32, kind="Internal").ap()

    A = lambda name, shape, dt: nc.alloc_sbuf_tensor(name, shape, dt).ap()

    c_ident = A("c_ident", [128, 128], F32)
    c_identb = A("c_identb", [128, 128], BF16)
    c_lt = A("c_lt", [128, 128], F32)
    c_convb = A("c_convb", [128, NCC_XBC], F32)
    c_b1 = A("c_b1", [128, NCC_FF], F32)
    c_dtb = A("c_dtb", [128, NH], F32)
    c_alog = A("c_alog", [128, NH], F32)
    c_arep = A("c_arep", [128, NH], F32)
    c_drep = A("c_drep", [128, NH], F32)
    c_normw = A("c_normw", [128, DI], F32)
    c_b2 = A("c_b2", [128, DM], F32)
    c_eps = A("c_eps", [128, 1], F32)
    c_one = A("c_one", [128, 1], F32)
    c_onesrow = A("c_onesrow", [1, 128], F32)
    tern_cols = A("tern_cols", [128, 4], F32)
    run_abs = A("run_abs", [128, 2], F32)
    tmp_col = A("tmp_col", [128, 2], F32)
    row4 = A("row4", [1, 4], F32)

    s_all = A("s_all", [128, DI], F32)
    xbcraw = A("xbcraw", [128, NCC_XBC * (T + 3)], BF16)

    x_sb = A("x_sb", [128, 1024], F32)
    x2_sb = A("x2_sb", [128, TT * 1024], F32)
    arena_bf = A("arena_bf", [128, 24576], BF16)
    arena_f = A("arena_f", [128, 6144], F32)
    warena = A("warena", [128, 2 * WH], BF16)
    e_bf = A("e_bf", [128, 2048], BF16)
    mt_bf = A("mt_bf", [128, 2048], BF16)
    xs_tm = A("xs_tm", [128, 2 * DI], BF16)
    b_tm = A("b_tm", [128, 2 * DS], BF16)
    xdt_bf = A("xdt_bf", [128, DI], BF16)
    wxdt_bf = A("wxdt_bf", [128, DI], BF16)
    s_bf = A("s_bf", [128, DI], BF16)
    ynfm = A("ynfm", [128, NCC_XS * T], BF16)
    ynbf_tm = A("ynbf_tm", [128, DI], BF16)
    dt_tm = A("dt_tm", [128, TT * NH], F32)
    bfc = A("b_fm", [128, T], BF16)
    cfc = A("c_fm", [128, T], BF16)
    acum4 = A("acum4", [128, TT * NH], F32)
    acum_fm = A("acum_fm", [32, 128], F32)
    gm4 = A("gm4", [128, TT * 128], F32)
    cq4 = A("cq4", [128, TT * NH], F32)
    halo_sb = A("halo_sb", [128, NCC_XBC * 3], BF16)
    clast_rep = A("clast_rep", [128, NH], F32)
    wdec = A("wdec", [128, NH], F32)
    bns = A("bns", [128, 48], F32)
    stat2 = A("stat2", [128, 8], F32)
    rcol = A("rcol", [128, 8], F32)
    lnstats = A("lnstats", [128, 2 * TT], F32)
    msq = A("msq", [128, 1], F32)
    dtu = A("dtu", [128, NH], F32)
    dtt = A("dtt", [128, NH], F32)

    xs_fm = arena_bf[:, 0:8192]
    sz_tm = arena_bf[:, 8192:16384]
    h1_fm = arena_bf[:, 16384:20480]
    h1bf_tm = arena_bf[:, 20480:24576]
    hg_fm = arena_bf[:, 0:16384]
    h2_fm = arena_bf[:, 16384:20480]
    h2bf_tm = arena_bf[:, 20480:24576]
    mt_bf2 = arena_bf[:, 16384:18432]
    xdt_bf2 = arena_bf[:, 18432:20480]
    sq_scr = arena_bf[:, 20480:22528]
    e_bf2 = arena_bf[:, 22528:24576]

    r_half = arena_f[:, 0:2048]
    r_slot = (r_half, None)
    y_sb = arena_f[:, 2048:4096]
    vc_sb = arena_f[:, 4096:5120]
    stmp = arena_f[:, 5120:6144]
    h1t_sb = arena_f[:, 0:1024]
    at_sb = arena_f[:, 1024:2048]
    f_sb = arena_f[:, 2048:3072]
    o_sb = arena_f[:, 3072:4096]
    stg_f = arena_f[:, 0:4096]
    s1_f = x2_sb[:, 0:4096]
    s2_f = ynfm.bitcast(F32)[:, 0:4096]
    stg_bf = arena_bf[:, 0:4096]
    r_slot = (r_half, xbcraw.bitcast(F32)[:, 0:2048])

    pbw = [nc.alloc_psum_tensor(f"pbw{i}", [128, 1024], F32).ap() for i in range(4)]
    pb = []
    for i in range(4):
        pb.append(pbw[i][:, 0:512])
        pb.append(pbw[i][:, 512:1024])

    def mm(out, lhsT, rhs, start, stop):
        def f(e, out=out, lhsT=lhsT, rhs=rhs, start=start, stop=stop):
            e.matmul(out=out, lhsT=lhsT, rhs=rhs, start=start, stop=stop)
        P.on("tensor", f)

    def mm_inc(out, lhsT, rhs, start, stop):
        tgt = P.opctr["tensor"].take(1)
        sem = P.esem["tensor"]

        def f(e, out=out, lhsT=lhsT, rhs=rhs, start=start, stop=stop, sem=sem):
            e.matmul(out=out, lhsT=lhsT, rhs=rhs, start=start, stop=stop).then_inc(sem, 1)
        P.on("tensor", f)
        return (sem, tgt)

    def tr_inc(out, in_, identity):
        tgt = P.opctr["tensor"].take(1)
        sem = P.esem["tensor"]

        def f(e, out=out, in_=in_, identity=identity, sem=sem):
            e.transpose(out=out, in_=in_, identity=identity).then_inc(sem, 1)
        P.on("tensor", f)
        return (sem, tgt)

    def v(fn):
        return P.dp("vector", fn)

    def a(fn):
        return P.dp("scalar", fn)

    def g(fn):
        return P.dp("gpsimd", fn)

    pbsig = [None] * 8

    def pb_wait(*banks):
        for b in set(banks):
            if pbsig[b] is not None:
                P.wait("tensor", pbsig[b])

    def pb_free(sig, *banks):
        for b in banks:
            pbsig[b] = sig

    def emit_ln_stats(src, sl=0):
        b0 = sl * 12
        s0 = sl * 2
        v(lambda e: e.bn_stats(out=bns[:, b0:b0 + 6], in_=src[:, 0:512]))
        v(lambda e: e.bn_stats(out=bns[:, b0 + 6:b0 + 12], in_=src[:, 512:1024]))
        sv = v(lambda e: e.bn_aggr(out=stat2[:, s0:s0 + 2], in_=bns[:, b0:b0 + 12]))
        P.wait("scalar", sv)
        a(lambda e: e.activation(out=rcol[:, s0:s0 + 1], in_=stat2[:, s0 + 1:s0 + 2],
                                 func=AF.Ln, bias=c_eps[:]))
        sa = a(lambda e: e.activation(out=rcol[:, s0:s0 + 1], in_=rcol[:, s0:s0 + 1],
                                      func=AF.Exp, scale=-0.5))
        P.wait("vector", sa)
        v(lambda e: e.scalar_tensor_tensor(out=rcol[:, s0 + 1:s0 + 2],
                                           in0=stat2[:, s0:s0 + 1], scalar=-1.0,
                                           in1=rcol[:, s0:s0 + 1], op0=ALU.mult, op1=ALU.mult))

    for dst, src in ((c_ident, ident_d), (c_identb, identb_d), (c_lt, lt_d), (c_convb, convb_d),
                     (c_b1, b1_d), (c_dtb, dtb_d), (c_alog, alog_d),
                     (c_drep, drep_d), (c_normw, normw_d), (c_b2, b2_d)):
        P.dma("gpsimd", dst[:], src)
    v(lambda e: e.memset(c_eps[:], 1e-5))
    v(lambda e: e.memset(c_one[:], 1.0))
    v(lambda e: e.memset(c_onesrow[:], 1.0))
    v(lambda e: e.memset(s_all[:], 0.0))
    v(lambda e: e.memset(run_abs[:], 0.0))
    v(lambda e: e.memset(halo_sb[:], 0.0))
    P.barrier()

    sa = a(lambda e: e.activation(out=c_arep[:], in_=c_alog[:], func=AF.Exp))
    P.wait("vector", sa)
    v(lambda e: e.tensor_scalar_mul(out=c_arep[:], in0=c_arep[:], scalar1=-1.0))

    cast_sig = {}
    csem = [nc.alloc_semaphore(f"cast{i}") for i in range(3)]
    cctr = [Ctr() for _ in range(3)]

    def cdma(grp, dst, src):
        return P.dma_sem("gpsimd", csem[grp], cctr[grp], dst, src)

    cast_sig["zdt"] = cdma(0, scr_wzdt, wzdt_d.rearrange("(k p) f -> k p f", k=NKC_DM))
    for cc in range(NCC_XBC):
        s = cdma(1, scr_wxbc[cc],
                 wxbc_d[:, cc * 128:(cc + 1) * 128]
                 .rearrange("(k p) f -> k p f", k=NKC_DM))
    for g3 in range(6):
        cast_sig[("xbc", g3)] = s
    for cc in range(NCC_XBC):
        cdma(2, scr_conv[cc], convdiag_d[cc].rearrange("p (k f) -> k p f", k=4))
    s2 = cdma(2, scr_wout, wout_d.rearrange("(k p) f -> k p f", k=NKC_DI))
    for gi2 in range(3):
        cast_sig[("conv", gi2)] = s2
    cast_sig["wout"] = s2

    gsig = {}
    p1_items = [(w1_d[kc * 128:(kc + 1) * 128, q * 1024:(q + 1) * 1024], 0)
                for kc in range(NKC_DM) for q in range(4)]
    p1_items += [(w2_d[kc * 128:(kc + 1) * 128, :], 1) for kc in range(NCC_FF)]
    p1_slots = [arena_f[:, i * 1024:(i + 1) * 1024] for i in range(4)]
    p1sem = [nc.alloc_semaphore(f"p1l{i}") for i in range(4)]
    p1ctr = [Ctr() for _ in range(4)]
    p1 = {"i": 0, "cons": {}, "add": {}}

    def drain_pass1(n):
        for _ in range(n):
            i = p1["i"]
            if i >= len(p1_items):
                return
            src, col = p1_items[i]
            sl = i % 4
            if i >= 4:
                P.wait("sync", p1["cons"][i - 4])
            ld = P.dma_sem("sync", p1sem[sl], p1ctr[sl], p1_slots[sl], src)
            P.wait_dma("scalar", ld)
            if i >= 2:
                P.wait("scalar", p1["add"][i - 2])
            sa = a(lambda e, sl=sl, i=i: e.activation(
                out=p1_slots[sl], in_=p1_slots[sl], func=AF.Abs,
                accum_out=tmp_col[:, i % 2:i % 2 + 1]))
            p1["cons"][i] = sa
            P.wait("vector", sa)
            p1["add"][i] = v(lambda e, col=col, i=i: e.tensor_add(
                out=run_abs[:, col:col + 1],
                in0=run_abs[:, col:col + 1], in1=tmp_col[:, i % 2:i % 2 + 1]))
            p1["i"] = i + 1
        if p1["i"] == len(p1_items):
            gsig["p1_done"] = p1["cons"][len(p1_items) - 1]

    def emit_thr():
        P.wait("tensor", p1["add"][len(p1_items) - 1])
        pb_wait(0)
        spe = mm_inc(pb[0][0:1, 0:2], c_one[:], run_abs[:, 0:2], True, True)
        P.wait("vector", spe)
        sv_r4 = v(lambda e: e.tensor_copy(out=row4[:, 0:2], in_=pb[0][0:1, 0:2]))
        pb_free(sv_r4, 0)
        v(lambda e: e.tensor_scalar_mul(out=row4[:, 0:1], in0=row4[:, 0:1],
                                        scalar1=1.0 / (FF * DM)))
        v(lambda e: e.tensor_scalar_mul(out=row4[:, 1:2], in0=row4[:, 1:2],
                                        scalar1=1.0 / (FF * DM)))
        v(lambda e: e.tensor_scalar_max(out=row4[:, 0:2], in0=row4[:, 0:2], scalar1=1e-8))
        v(lambda e: e.tensor_scalar_mul(out=row4[:, 0:2], in0=row4[:, 0:2], scalar1=0.5))
        v(lambda e: e.tensor_copy(out=row4[:, 2:3], in_=row4[:, 1:2]))
        v(lambda e: e.tensor_scalar_mul(out=row4[:, 1:2], in0=row4[:, 0:1], scalar1=-1.0))
        sv = v(lambda e: e.tensor_scalar_mul(out=row4[:, 3:4], in0=row4[:, 2:3], scalar1=-1.0))
        P.wait("tensor", sv)
        pb_wait(1)
        spe2 = mm_inc(pb[1][:, 0:4], c_onesrow[:], row4[:, 0:4], True, True)
        P.wait("vector", spe2)
        sv_tc = v(lambda e: e.tensor_copy(out=tern_cols[:], in_=pb[1][:, 0:4]))
        pb_free(sv_tc, 1)
        gsig["tern_cols"] = sv_tc

    p2_items = [(w1_d[kc * 128:(kc + 1) * 128, q * 1024:(q + 1) * 1024], 0, 1,
                 scr_t1[q * 8:(q + 1) * 8, kc], True)
                for kc in range(NKC_DM) for q in range(4)]
    p2_items += [(w2_d[kc * 128:(kc + 1) * 128, :], 2, 3, scr_t2[kc], False)
                 for kc in range(NCC_FF)]
    p2_lslots = [x2_sb[:, i * 1024:(i + 1) * 1024] for i in range(4)]
    p2_s1 = (e_bf[:, 0:1024], e_bf[:, 1024:2048])
    p2_s2 = (mt_bf[:, 0:1024], mt_bf[:, 1024:2048])
    p2sem = [nc.alloc_semaphore(f"p2l{i}") for i in range(4)]
    p2ctr = [Ctr() for _ in range(4)]
    p2 = {"i": 0, "cons": {}, "add": {}, "st": {}}

    def drain_pass2(n):
        for _ in range(n):
            i = p2["i"]
            if i >= len(p2_items):
                return
            src, thr_i, nthr_i, dst, is_t1 = p2_items[i]
            sl = i % 4
            s12 = i % 2
            qeng = "sync" if i % 2 == 0 else "gpsimd"
            if i >= 4:
                P.wait(qeng, p2["cons"][i - 4])
            ld = P.dma_sem(qeng, p2sem[sl], p2ctr[sl], p2_lslots[sl], src)
            P.wait_dma("scalar", ld)
            if i == 0:
                P.wait("scalar", gsig["tern_cols"])
            if i >= 2:
                P.wait("scalar", p2["add"][i - 2])
                P.wait_dma("scalar", p2["st"][i - 2])
            a(lambda e, sl=sl, s12=s12, nthr_i=nthr_i: e.activation(
                out=p2_s1[s12], in_=p2_lslots[sl], func=AF.Sign,
                bias=tern_cols[:, nthr_i:nthr_i + 1]))
            sa = a(lambda e, sl=sl, s12=s12, thr_i=thr_i: e.activation(
                out=p2_s2[s12], in_=p2_lslots[sl], func=AF.Sign,
                bias=tern_cols[:, thr_i:thr_i + 1]))
            p2["cons"][i] = sa
            P.wait("vector", sa)
            if i >= 2:
                P.wait_dma("vector", p2["st"][i - 2])
            sv = v(lambda e, s12=s12: e.tensor_add(out=p2_s1[s12], in0=p2_s1[s12],
                                                   in1=p2_s2[s12]))
            p2["add"][i] = sv
            P.wait("gpsimd", sv)
            if is_t1:
                dst = dst.rearrange("c p f -> p c f")
                srcap = p2_s1[s12].rearrange("p (c f) -> p c f", c=8)
            else:
                srcap = p2_s1[s12]
            p2["st"][i] = P.dma_sem("gpsimd", P.s2sem[s12], P.s2ctr[s12], dst, srcap)
            p2["i"] = i + 1
        if p2["i"] == len(p2_items):
            n2 = len(p2_items)
            gsig["t2_done"] = (p2["st"][n2 - 2], p2["st"][n2 - 1])
            gsig["t1_done"] = gsig["t2_done"]
            gsig["p2_cons_done"] = p2["cons"][n2 - 1]

    total_sc = SEQ_PER_CORE * NSC_PER_SEQ if n_sc is None else n_sc
    x_slots = (x_sb, arena_f[:, 4096:5120])

    def emit_p1(seq, tok0, act_gate, xbar_gate, xload_gate):
        prev = {}
        h1t_tgt = {}
        for t in range(TT):
            sl = t % 2
            xs_ = x_slots[sl]
            if t < 2:
                if xload_gate is not None:
                    P.wait("gpsimd", xload_gate)
            else:
                P.wait("gpsimd", prev[t - 2])
            s_x = P.dma_sem("gpsimd", P.xlsem[sl], P.xlctr[sl], xs_[:],
                            x_d[seq, tok0 + t * 128: tok0 + (t + 1) * 128, :])
            P.wait_dma("vector", s_x)
            emit_ln_stats(xs_, sl)
            v(lambda e, t=t, sl=sl: e.tensor_copy(out=lnstats[:, t:t + 1],
                                                  in_=rcol[:, 2 * sl:2 * sl + 1]))
            v(lambda e, t=t, sl=sl: e.tensor_copy(out=lnstats[:, TT + t:TT + t + 1],
                                                  in_=rcol[:, 2 * sl + 1:2 * sl + 2]))
            sv_nm = P.sig("vector")
            P.wait("scalar", sv_nm)
            if t == 0 and act_gate is not None:
                for ag in act_gate:
                    P.wait("scalar", ag)
            sv = a(lambda e, t=t, sl=sl, xs_=xs_: e.activation(
                out=h1bf_tm[:, t * 1024:(t + 1) * 1024], in_=xs_[:], func=AF.Identity,
                scale=rcol[:, 2 * sl:2 * sl + 1], bias=rcol[:, 2 * sl + 1:2 * sl + 2]))
            prev[t] = sv
            P.wait("sync", sv)
            if t == 0 and xbar_gate is not None:
                P.wait("sync", xbar_gate)
            h_prev = P.hctr[sl].v
            for kc in range(NKC_DM):
                P.dma_sem("sync", P.hsem[sl], P.hctr[sl],
                          h1_fm[:, kc * 512 + t * 128: kc * 512 + (t + 1) * 128],
                          h1bf_tm[:, t * 1024 + kc * 128: t * 1024 + (kc + 1) * 128],
                          transpose=True, after_sem=(h_prev if kc == 0 else 0))
            h1t_tgt[t] = (P.hsem[sl], P.hctr[sl].v)
        return h1t_tgt

    h1_done = emit_p1(0, 0, None, None, None)
    drain_pass1(len(p1_items))
    emit_thr()

    for sc_i in range(total_sc):
        seq = sc_i // NSC_PER_SEQ
        sci = sc_i % NSC_PER_SEQ
        tok0 = sci * T
        if sci == 0 and sc_i > 0:
            v(lambda e: e.memset(s_all[:], 0.0))
            v(lambda e: e.memset(halo_sb[:], 0.0))
            P.barrier()

        reqs = []
        for gi in range(5):
            ncols = 512 if gi < 4 else DTP
            reqs.append((NKC_DM * ncols, ("k", NKC_DM),
                         scr_wzdt[:, :, gi * 512: gi * 512 + ncols].rearrange("k p f -> p k f")))
        for gx in range(6):
            reqs.append((3 * NKC_DM * 128, ("ck", 3, NKC_DM),
                         scr_wxbc[gx * 3:(gx + 1) * 3].rearrange("c k p f -> p c k f")))
        for c0, ncc in ((0, 8), (8, 8), (16, 2)):
            reqs.append((ncc * 4 * 128, ("ck", ncc, 4),
                         scr_conv[c0:c0 + ncc].rearrange("c k p f -> p c k f")))
        for gx in range(4):
            reqs.append((4 * 1024, ("k", 4),
                         scr_wout[gx * 4:(gx + 1) * 4].rearrange("k p f -> p k f")))
        for gx in range(8):
            reqs.append((4 * NKC_DM * 128, ("ck", 4, NKC_DM),
                         scr_t1[gx * 4:(gx + 1) * 4].rearrange("c k p f -> p c k f")))
        for gx in range(8):
            reqs.append((4 * 1024, ("k", 4),
                         scr_t2[gx * 4:(gx + 1) * 4].rearrange("k p f -> p k f")))

        ring = {"issued": 0, "tgts": {}, "cons": {}}
        if sc_i == 0:
            req_gates = ([cast_sig["zdt"]] * 5
                         + [cast_sig[("xbc", g)] for g in range(6)]
                         + [cast_sig[("conv", g)] for g in range(3)]
                         + [cast_sig["wout"]] * 4
                         + ["t1_done"] * 8 + ["t2_done"] * 8)
        else:
            req_gates = None

        def ring_issue_one(ring=ring, reqs=reqs, req_gates=req_gates):
            kk = ring["issued"]
            if kk >= len(reqs):
                return
            cols, view, src = reqs[kk]
            if req_gates is not None:
                gk = req_gates[kk]
                if isinstance(gk, str):
                    gk = gsig[gk]
                if isinstance(gk, tuple) and gk and isinstance(gk[0], tuple):
                    P.wait("scalar", *gk)
                else:
                    P.wait("scalar", gk)
            if kk >= 2:
                P.wait("scalar", ring["cons"][kk - 2])
            slot = kk % 2
            base = slot * WH
            dst = warena[:, base:base + cols]
            if view[0] == "k":
                dst = dst.rearrange("p (k f) -> p k f", k=view[1])
            else:
                dst = dst.rearrange("p (c k f) -> p c k f", c=view[1], k=view[2])
            ring["tgts"][kk] = P.dma_sem("scalar", P.wsem[slot], P.wctr[slot], dst, src)
            ring["issued"] += 1

        def ring_use(kk, ring=ring):
            P.wait_dma("tensor", ring["tgts"][kk])
            return (kk % 2) * WH

        def ring_done(kk, sig, ring=ring):
            ring["cons"][kk] = sig
            ring_issue_one()

        ring_issue_one()
        ring_issue_one()

        for gi in range(5):
            ncols = 512 if gi < 4 else DTP
            base = ring_use(gi)
            pe_sigs = []
            for t in range(TT):
                bidx = t if gi < 4 else 4 + t
                bank = pb[bidx]
                pb_wait(bidx)
                if gi == 0:
                    P.wait_dma("tensor", h1_done[t])
                for kc in range(NKC_DM):
                    args = (bank[:, 0:ncols],
                            h1_fm[:, kc * 512 + t * 128: kc * 512 + (t + 1) * 128],
                            warena[:, base + kc * ncols: base + (kc + 1) * ncols])
                    if kc < NKC_DM - 1:
                        mm(*args, start=(kc == 0), stop=False)
                    else:
                        pe_sigs.append(mm_inc(*args, start=False, stop=True))
            ring_done(gi, pe_sigs[-1])
            if gi < 4:
                for t in range(TT):
                    P.wait("scalar", pe_sigs[t])
                    sa = a(lambda e, t=t, gi=gi: e.activation(
                        out=sz_tm[:, t * DI + gi * 512: t * DI + (gi + 1) * 512],
                        in_=pb[t][:, 0:512], func=AF.Silu))
                    pb_free(sa, t)
            else:
                for t in range(TT):
                    P.wait("vector", pe_sigs[t])
                    sv = v(lambda e, t=t: e.tensor_add(out=dtu[:], in0=pb[4 + t][:, 0:DTP],
                                                       in1=c_dtb[:]))
                    pb_free(sv, 4 + t)
                    P.wait("scalar", sv)
                    a(lambda e: e.activation(out=dtt[:], in_=dtu[:], func=AF.Abs))
                    a(lambda e: e.activation(out=dtt[:], in_=dtt[:], func=AF.Exp, scale=-1.0))
                    sa = a(lambda e: e.activation(out=dtt[:], in_=dtt[:], func=AF.Ln,
                                                  bias=c_one[:]))
                    P.wait("vector", sa)
                    v(lambda e: e.tensor_relu(out=dtu[:], in_=dtu[:]))
                    v(lambda e, t=t: e.tensor_add(out=dt_tm[:, t * NH:(t + 1) * NH],
                                                  in0=dtt[:], in1=dtu[:]))
            if sc_i == 0:
                drain_pass2(8)

        sv_halo_head = v(lambda e: e.tensor_copy(
            out=xbcraw[:].rearrange("p (c w) -> p c w", c=NCC_XBC)[:, :, 0:3],
            in_=halo_sb[:].rearrange("p (c w) -> p c w", c=NCC_XBC)))
        p4sig = {}
        for gx in range(6):
            base = ring_use(5 + gx)
            last = None
            for i in range(3):
                cc = gx * 3 + i
                b = cc % 2
                pb_wait(b)
                for kc in range(NKC_DM):
                    args = (pb[b][:, 0:512],
                            warena[:, base + (i * NKC_DM + kc) * 128:
                                   base + (i * NKC_DM + kc + 1) * 128],
                            h1_fm[:, kc * 512:(kc + 1) * 512])
                    if kc < NKC_DM - 1:
                        mm(*args, start=(kc == 0), stop=False)
                    else:
                        last = mm_inc(*args, start=False, stop=True)
                P.wait("scalar", last)
                sa = a(lambda e, cc=cc, b=b: e.activation(
                    out=xbcraw[:, cc * (T + 3) + 3: cc * (T + 3) + 3 + T],
                    in_=pb[b][:, 0:512], func=AF.Identity))
                p4sig[cc] = sa
                pb_free(sa, b)
            ring_done(5 + gx, last)
            if sc_i == 0:
                drain_pass2(8)

        for gi2, (c0, ncc) in enumerate(((0, 8), (8, 8), (16, 2))):
            base = ring_use(11 + gi2)
            last = None
            for i in range(ncc):
                cc = c0 + i
                b = 4 + (cc % 2)
                pb_wait(b)
                P.wait("tensor", p4sig[cc])
                if cc == 0:
                    P.wait("tensor", sv_halo_head)
                for k in range(4):
                    args = (pb[b][:, 0:512],
                            warena[:, base + (i * 4 + k) * 128: base + (i * 4 + k + 1) * 128],
                            xbcraw[:, cc * (T + 3) + k: cc * (T + 3) + k + T])
                    if k < 3:
                        mm(*args, start=(k == 0), stop=False)
                    else:
                        last = mm_inc(*args, start=False, stop=True)
                P.wait("scalar", last)
                if cc < NCC_XS:
                    dst = xs_fm[:, cc * T:(cc + 1) * T]
                elif cc == NCC_XS:
                    dst = bfc[:]
                else:
                    dst = cfc[:]
                sa = a(lambda e, dst=dst, b=b, cc=cc: e.activation(
                    out=dst, in_=pb[b][:, 0:512], func=AF.Silu,
                    bias=c_convb[:, cc:cc + 1]))
                pb_free(sa, b)
            ring_done(11 + gi2, last)
        conv_done = P.sig("scalar")
        P.wait("vector", P.sig("scalar"))
        sv_tail = v(lambda e: e.tensor_copy(
            out=halo_sb[:].rearrange("p (c w) -> p c w", c=NCC_XBC),
            in_=xbcraw[:].rearrange("p (c w) -> p c w", c=NCC_XBC)[:, :, T:T + 3]))
        if sc_i == 0:
            drain_pass2(len(p2_items))

        pe_after_p5 = P.sig("tensor")

        st_tgts = {}
        for qi in range(TT):
            sv_a = v(lambda e, qi=qi: e.tensor_mul(out=dtu[:],
                                                   in0=dt_tm[:, qi * NH:(qi + 1) * NH],
                                                   in1=c_arep[:]))
            P.wait("tensor", sv_a)
            pb_wait(4)
            spe_lt = mm_inc(pb[4][:, 0:NH], c_lt[:], dtu[:], True, True)
            P.wait("vector", spe_lt)
            sv_ac = v(lambda e, qi=qi: e.tensor_copy(out=acum4[:, qi * NH:(qi + 1) * NH],
                                                     in_=pb[4][:, 0:NH]))
            P.wait("tensor", sv_ac)
            spe_tr = tr_inc(pb[4][0:32, 128:256], acum4[:, qi * NH:(qi + 1) * NH], c_ident[:])
            P.wait("scalar", spe_tr)
            if qi > 0:
                P.wait_dma("scalar", st_tgts[qi - 1])
            sa_fm = a(lambda e: e.activation(out=acum_fm[:], in_=pb[4][0:32, 128:256],
                                             func=AF.Identity))
            a(lambda e, qi=qi: e.activation(out=cq4[:, qi * NH:(qi + 1) * NH],
                                            in_=acum4[:, qi * NH:(qi + 1) * NH], func=AF.Exp))
            pb_free(sa_fm, 4)
            st_tgts[qi] = P.dma_sem("scalar", P.rbsem, P.rbctr,
                                    scr_ac[qi].rearrange("(h i) -> h i", h=32), acum_fm[:],
                                    after_own=True)
            pb_wait(5)
            if qi == 0:
                P.wait("tensor", conv_done)
            spe_g = mm_inc(pb[5][:, 0:128], bfc[:, qi * 128:(qi + 1) * 128],
                           cfc[:, qi * 128:(qi + 1) * 128], True, True)
            P.wait("vector", spe_g)
            sv_gm = v(lambda e, qi=qi: e.tensor_mul(out=gm4[:, qi * 128:(qi + 1) * 128],
                                                    in0=pb[5][:, 0:128], in1=c_lt[:]))
            pb_free(sv_gm, 5)
        prep_gm = P.sig("vector")
        prep_cq = P.sig("scalar")

        xsxb_tgt = {}
        xs_free = {}

        def emit_xs_xbars(qi):
            sl = qi % 2
            if qi >= 2:
                P.wait("sync", xs_free[qi - 2])
            for cc in range(NCC_XS):
                P.dma_sem("sync", P.xssem[sl], P.xsctr[sl],
                          xs_tm[:, sl * DI + cc * 128: sl * DI + (cc + 1) * 128],
                          xs_fm[:, cc * T + qi * 128: cc * T + (qi + 1) * 128], transpose=True)
            P.dma_sem("sync", P.xssem[sl], P.xsctr[sl],
                      b_tm[:, sl * DS:(sl + 1) * DS],
                      bfc[:, qi * 128:(qi + 1) * 128], transpose=True)
            xsxb_tgt[qi] = (P.xssem[sl], P.xsctr[sl].v)

        P.wait("sync", conv_done)
        emit_xs_xbars(0)
        emit_xs_xbars(1)

        r_tgt = {}
        r_free = {}

        def emit_r_load(j):
            qi2, half2 = divmod(j, 2)
            sl = j % 2
            if j >= 2:
                P.wait("scalar", r_free[j - 2][0], r_free[j - 2][1])
            if sc_i == 0 and j < 2:
                P.wait("scalar", gsig["p1_done"])
            if sl == 1 and j == 1:
                P.wait("scalar", sv_tail, pe_after_p5)
            P.wait_dma("scalar", st_tgts[qi2])
            r_tgt[j] = P.dma_sem("scalar", P.rsem[sl], P.rctr[sl], r_slot[sl][:],
                                 scr_ac[qi2][half2 * 2048:(half2 + 1) * 2048]
                                 .unsqueeze(0).broadcast_to([128, 2048]))

        emit_r_load(0)
        emit_r_load(1)

        e_sl = (e_bf, e_bf2)
        mt_sl = (mt_bf, mt_bf2)
        xdt_sl = (xdt_bf, xdt_bf2)
        spe_half = {}
        sg_mt_s = {}
        sg_wx_s = {}
        sg_xdt_s = {}
        sa_cl_s = {}
        sbf_cast = {}
        sg_ss_par = {0: None, 1: None}
        sv_sadd_s = {}
        sg_d1_prev = None
        sv_yadd2_prev = None
        ynxb_prev = None

        def emit_echain(j):
            qi2, half2 = divmod(j, 2)
            h0 = half2 * 16
            rb = r_slot[j % 2]
            P.wait_dma("scalar", r_tgt[j])
            if sg_ss_par[half2] is not None:
                P.wait("scalar", sg_ss_par[half2])
            sa_cl = a(lambda e, h0=h0, rb=rb: e.activation(
                out=clast_rep[:, h0:h0 + 16],
                in_=rb[:].rearrange("p (h i) -> p h i", h=16)[:, :, 127], func=AF.Exp))
            sa_cl_s[j] = sa_cl
            P.wait("vector", sa_cl)
            sv_d = v(lambda e, h0=h0, qi2=qi2, rb=rb: e.tensor_sub(
                out=rb[:].rearrange("p (h i) -> p h i", h=16),
                in0=rb[:].rearrange("p (h i) -> p h i", h=16),
                in1=acum4[:, qi2 * NH + h0: qi2 * NH + h0 + 16]
                .unsqueeze(2).broadcast_to([128, 16, 128])))
            P.wait("scalar", sv_d)
            if sg_wx_s.get(j - 2) is not None:
                P.wait("scalar", sg_wx_s[j - 2])
            a(lambda e, h0=h0, rb=rb: e.activation(
                out=wdec[:, h0:h0 + 16],
                in_=rb[:].rearrange("p (h i) -> p h i", h=16)[:, :, 127], func=AF.Exp))
            if sg_mt_s.get(j - 2) is not None:
                P.wait("scalar", sg_mt_s[j - 2])
            if sc_i == 0 and j == 0:
                P.wait("scalar", *gsig["t2_done"])
            esl = e_sl[j % 2]
            sa_e = a(lambda e, rb=rb, esl=esl: e.activation(out=esl[:], in_=rb[:],
                                                            func=AF.Exp))
            P.wait("vector", sa_e)
            sv_e = v(lambda e, esl=esl: e.tensor_scalar_min(out=esl[:], in0=esl[:],
                                                            scalar1=1e30))
            r_free[j] = (sa_e, sv_d)
            if j + 2 < 2 * TT:
                emit_r_load(j + 2)
            return sv_e

        def emit_mtwx(j, sv_e):
            qi2, half2 = divmod(j, 2)
            h0 = half2 * 16
            P.wait("gpsimd", sv_e)
            if j == 0:
                P.wait("gpsimd", prep_gm)
                if sc_i == 0:
                    P.wait("gpsimd", *gsig["t2_done"])
            if spe_half.get(j - 2) is not None:
                P.wait("gpsimd", spe_half[j - 2])
            esl = e_sl[j % 2]
            msl = mt_sl[j % 2]
            xsl = xdt_sl[qi2 % 2]
            sg_mt = g(lambda e, qi2=qi2, esl=esl, msl=msl: e.tensor_tensor(
                out=msl[:].rearrange("p (h i) -> p h i", h=16),
                in0=esl[:].rearrange("p (h i) -> p h i", h=16),
                in1=gm4[:, qi2 * 128:(qi2 + 1) * 128].unsqueeze(1)
                .broadcast_to([128, 16, 128]),
                op=ALU.mult))
            sg_wx = g(lambda e, h0=h0, xsl=xsl: e.tensor_tensor(
                out=wxdt_bf[:, h0 * HD:(h0 + 16) * HD].rearrange("p (h w) -> p h w", h=16),
                in0=xsl[:, h0 * HD:(h0 + 16) * HD].rearrange("p (h w) -> p h w", h=16),
                in1=wdec[:, h0:h0 + 16].unsqueeze(2).broadcast_to([128, 16, HD]),
                op=ALU.mult))
            sg_mt_s[j] = sg_mt
            sg_wx_s[j] = sg_wx

        def emit_xdt(qi):
            P.wait_dma("gpsimd", xsxb_tgt[qi])
            if spe_half.get(2 * qi - 3) is not None:
                P.wait("gpsimd", spe_half[2 * qi - 3])
            sl_q = qi % 2
            xsl = xdt_sl[sl_q]
            sg_xdt = g(lambda e, qi=qi, sl_q=sl_q, xsl=xsl: e.tensor_tensor(
                out=xsl[:].rearrange("p (h w) -> p h w", h=NH),
                in0=xs_tm[:, sl_q * DI:(sl_q + 1) * DI].rearrange("p (h w) -> p h w", h=NH),
                in1=dt_tm[:, qi * NH:(qi + 1) * NH].unsqueeze(2).broadcast_to([128, NH, HD]),
                op=ALU.mult))
            sg_xdt_s[qi] = sg_xdt

        def emit_sbf_cast(qi, half):
            h0 = half * 16
            sv_c = v(lambda e, h0=h0: e.tensor_copy(
                out=s_bf[:, h0 * HD:(h0 + 16) * HD], in_=s_all[:, h0 * HD:(h0 + 16) * HD]))
            sbf_cast[(qi, half)] = sv_c

        emit_xdt(0)
        emit_mtwx(0, emit_echain(0))
        emit_sbf_cast(0, 0)
        emit_sbf_cast(0, 1)

        for qi in range(TT):
            sl_q = qi % 2
            sv_y1_last = None
            for half in range(2):
                j = 2 * qi + half
                h0 = half * 16
                if j + 1 < 2 * TT:
                    qn, hn = divmod(j + 1, 2)
                    if hn == 0:
                        emit_xdt(qn)
                    emit_mtwx(j + 1, emit_echain(j + 1))
                Yb = (0, 1) if half == 0 else (6, 7)
                Vb = (2, 3) if half == 0 else (0, 1)
                Sb = (4, 5) if half == 0 else (2, 3)
                Ypw = pbw[Yb[0] // 2]
                Vpw = pbw[Vb[0] // 2]
                Spw = pbw[Sb[0] // 2]
                msl = mt_sl[j % 2]
                xsl = xdt_sl[sl_q]
                P.wait("tensor", sg_mt_s[j])
                if half == 0:
                    P.wait("tensor", sg_xdt_s[qi])
                pb_wait(*Yb)
                for hl in range(16):
                    h = h0 + hl
                    bank = pb[Yb[hl // 8]]
                    mm(bank[:, (hl % 8) * 64:(hl % 8 + 1) * 64],
                       msl[:, hl * 128:(hl + 1) * 128],
                       xsl[:, h * HD:(h + 1) * HD],
                       start=(hl % 8 == 0), stop=(hl % 8 == 7))
                pb_wait(*Vb)
                P.wait("tensor", sbf_cast[(qi, half)])
                for b2 in range(2):
                    mm(pb[Vb[b2]][:, 0:512],
                       cfc[:, qi * 128:(qi + 1) * 128],
                       s_bf[:, (h0 + b2 * 8) * HD:(h0 + (b2 + 1) * 8) * HD],
                       start=True, stop=True)
                pb_wait(*Sb)
                if half == 0:
                    P.wait_dma("tensor", xsxb_tgt[qi])
                P.wait("tensor", sg_wx_s[j])
                spe_all = None
                for b2 in range(2):
                    args = (pb[Sb[b2]][:, 0:512],
                            b_tm[:, sl_q * DS:(sl_q + 1) * DS],
                            wxdt_bf[:, (h0 + b2 * 8) * HD:(h0 + (b2 + 1) * 8) * HD])
                    if b2 == 0:
                        mm(*args, start=True, stop=True)
                    else:
                        spe_all = mm_inc(*args, start=True, stop=True)
                spe_half[j] = spe_all
                P.wait("vector", spe_all)
                if j == 0:
                    P.wait("vector", prep_cq)
                if half == 0 and sg_d1_prev is not None:
                    P.wait("vector", sg_d1_prev)
                sv_vc = v(lambda e, qi=qi, h0=h0, Vpw=Vpw: e.tensor_mul(
                    out=vc_sb[:].rearrange("p (h w) -> p h w", h=16),
                    in0=Vpw[:].rearrange("p (h w) -> p h w", h=16),
                    in1=cq4[:, qi * NH + h0: qi * NH + h0 + 16]
                    .unsqueeze(2).broadcast_to([128, 16, HD])))
                pb_free(sv_vc, *Vb)
                sv_y1 = v(lambda e, half=half, Ypw=Ypw: e.tensor_add(
                    out=y_sb[:, half * 1024:(half + 1) * 1024], in0=Ypw[:], in1=vc_sb[:]))
                pb_free(sv_y1, *Yb)
                P.wait("gpsimd", sa_cl_s[j])
                if sv_sadd_s.get(j - 1) is not None:
                    P.wait("gpsimd", sv_sadd_s[j - 1])
                if half == 0 and sv_yadd2_prev is not None:
                    P.wait("gpsimd", sv_yadd2_prev)
                sg_ss = g(lambda e, h0=h0: e.tensor_tensor(
                    out=stmp[:].rearrange("p (h w) -> p h w", h=16),
                    in0=s_all[:, h0 * HD:(h0 + 16) * HD].rearrange("p (h w) -> p h w", h=16),
                    in1=clast_rep[:, h0:h0 + 16].unsqueeze(2).broadcast_to([128, 16, HD]),
                    op=ALU.mult))
                sg_ss_par[half] = sg_ss
                P.wait("vector", sg_ss)
                sv_sadd = v(lambda e, h0=h0, Spw=Spw: e.tensor_add(
                    out=s_all[:, h0 * HD:(h0 + 16) * HD], in0=stmp[:], in1=Spw[:]))
                pb_free(sv_sadd, *Sb)
                sv_sadd_s[j] = sv_sadd
                if qi + 1 < TT:
                    emit_sbf_cast(qi + 1, half)
                sv_y1_last = sv_y1

            P.wait("gpsimd", sv_y1_last, sv_sadd_s[2 * qi + 1])
            sg_d1 = g(lambda e, sl_q=sl_q: e.tensor_tensor(
                out=vc_sb[:].rearrange("p (h w) -> p h w", h=16),
                in0=xs_tm[:, sl_q * DI: sl_q * DI + 1024].rearrange("p (h w) -> p h w", h=16),
                in1=c_drep[:, 0:16].unsqueeze(2).broadcast_to([128, 16, HD]), op=ALU.mult))
            sg_d1_prev = sg_d1
            sg_d2 = g(lambda e, sl_q=sl_q: e.tensor_tensor(
                out=stmp[:].rearrange("p (h w) -> p h w", h=16),
                in0=xs_tm[:, sl_q * DI + 1024: sl_q * DI + 2048]
                .rearrange("p (h w) -> p h w", h=16),
                in1=c_drep[:, 16:32].unsqueeze(2).broadcast_to([128, 16, HD]), op=ALU.mult))
            xs_free[qi] = sg_d2
            if qi + 2 < TT:
                emit_xs_xbars(qi + 2)
            P.wait("vector", sg_d2)
            v(lambda e: e.tensor_add(out=y_sb[:, 0:1024], in0=y_sb[:, 0:1024], in1=vc_sb[:]))
            sv_yadd2 = v(lambda e: e.tensor_add(out=y_sb[:, 1024:2048],
                                                in0=y_sb[:, 1024:2048], in1=stmp[:]))
            sv_yadd2_prev = sv_yadd2
            sv_gate = v(lambda e, qi=qi: e.tensor_mul(out=y_sb[:], in0=y_sb[:],
                                                      in1=sz_tm[:, qi * DI:(qi + 1) * DI]))
            sv_sq = v(lambda e: e.scalar_tensor_tensor(
                out=sq_scr[:], in0=y_sb[:], scalar=1.0, in1=y_sb[:],
                op0=ALU.mult, op1=ALU.mult, accum_out=msq[:]))
            P.wait("scalar", sv_sq)
            a(lambda e: e.activation(out=msq[:], in_=msq[:], func=AF.Ln, bias=c_eps[:],
                                     scale=1.0 / DI))
            sa_r = a(lambda e: e.activation(out=msq[:], in_=msq[:], func=AF.Exp, scale=-0.5))
            P.wait("vector", sa_r)
            if ynxb_prev is not None:
                for yp in ynxb_prev:
                    P.wait_dma("vector", yp)
            sv_yn = v(lambda e: e.scalar_tensor_tensor(
                out=ynbf_tm[:], in0=y_sb[:], scalar=msq[:], in1=c_normw[:],
                op0=ALU.mult, op1=ALU.mult))
            P.wait("sync", sv_yn)
            for cc in range(NCC_XS):
                gidx = cc // 4
                P.dma_sem("sync", P.ynsem[gidx], P.ynctr[gidx],
                          ynfm[:, cc * T + qi * 128: cc * T + (qi + 1) * 128],
                          ynbf_tm[:, cc * 128:(cc + 1) * 128], transpose=True)
            ynxb_prev = tuple((P.ynsem[i], P.ynctr[i].v) for i in range(4))

        if sc_i == 0:
            drain_pass2(len(p2_items))
        yn_tgts = tuple((P.ynsem[i], P.ynctr[i].v) for i in range(4))
        p8sig = {}
        for gx in range(4):
            base = ring_use(14 + gx)
            P.wait_dma("tensor", yn_tgts[gx])
            if gx == 0:
                pb_wait(*range(8))
            last = None
            for t in range(TT):
                for i in range(4):
                    kc = gx * 4 + i
                    for b2 in range(2):
                        args = (pb[2 * t + b2][:, 0:512],
                                ynfm[:, kc * T + t * 128: kc * T + (t + 1) * 128],
                                warena[:, base + i * 1024 + b2 * 512:
                                       base + i * 1024 + (b2 + 1) * 512])
                        start = (gx == 0 and i == 0)
                        stop = (gx == 3 and i == 3)
                        group_last = (t == TT - 1 and i == 3 and b2 == 1)
                        if stop or group_last:
                            s = mm_inc(*args, start=start, stop=stop)
                            if stop:
                                p8sig[(t, b2)] = s
                            last = s
                        else:
                            mm(*args, start=start, stop=False)
            ring_done(14 + gx, last)
        h2t = {}
        at_slots = (at_sb, f_sb)
        h1t_slots = (h1t_sb, o_sb)
        sgx2 = {}
        for t in range(TT):
            sl = t % 2
            rsl = 2 + sl
            xs_ = x_slots[sl]
            if t >= 2:
                P.wait("gpsimd", sgx2[t - 2])
            s_x = P.dma_sem("gpsimd", P.xlsem[sl], P.xlctr[sl], xs_[:],
                            x_d[seq, tok0 + t * 128: tok0 + (t + 1) * 128, :])
            P.wait_dma("scalar", s_x)
            sa_h1t = a(lambda e, t=t, xs_=xs_, sl=sl: e.activation(
                out=h1t_slots[sl][:], in_=xs_[:], func=AF.Identity,
                scale=lnstats[:, t:t + 1], bias=lnstats[:, TT + t:TT + t + 1]))
            P.wait("vector", p8sig[(t, 0)], p8sig[(t, 1)], sa_h1t)
            sv_at = v(lambda e, t=t, sl=sl: e.tensor_add(
                out=at_slots[sl][:], in0=h1t_slots[sl][:], in1=pbw[t][:]))
            pb_free(sv_at, 2 * t, 2 * t + 1)
            emit_ln_stats(at_slots[sl], rsl)
            sv_nm = P.sig("vector")
            P.wait("scalar", sv_nm)
            sa_ln = a(lambda e, sl=sl, rsl=rsl: e.activation(
                out=at_slots[sl][:], in_=at_slots[sl][:], func=AF.Identity,
                scale=rcol[:, 2 * rsl:2 * rsl + 1], bias=rcol[:, 2 * rsl + 1:2 * rsl + 2]))
            P.wait("gpsimd", sa_ln)
            if sc_i == 0 and t == 0:
                P.wait("gpsimd", gsig["p2_cons_done"])
            sg_x2 = g(lambda e, t=t, sl=sl, xs_=xs_: e.tensor_add(
                out=x2_sb[:, t * 1024:(t + 1) * 1024], in0=xs_[:], in1=at_slots[sl][:]))
            sgx2[t] = sg_x2
            P.wait("vector", sg_x2)
            emit_ln_stats(x2_sb[:, t * 1024:(t + 1) * 1024], rsl)
            sv_nm2a = P.sig("vector")
            P.wait("scalar", sv_nm2a)
            sv_nm2 = a(lambda e, t=t, rsl=rsl: e.activation(
                out=h2bf_tm[:, t * 1024:(t + 1) * 1024],
                in_=x2_sb[:, t * 1024:(t + 1) * 1024], func=AF.Identity,
                scale=rcol[:, 2 * rsl:2 * rsl + 1], bias=rcol[:, 2 * rsl + 1:2 * rsl + 2]))
            P.wait("sync", sv_nm2)
            h_prev = P.hctr[sl].v
            for kc in range(NKC_DM):
                P.dma_sem("sync", P.hsem[sl], P.hctr[sl],
                          h2_fm[:, kc * 512 + t * 128: kc * 512 + (t + 1) * 128],
                          h2bf_tm[:, t * 1024 + kc * 128: t * 1024 + (kc + 1) * 128],
                          transpose=True, after_sem=(h_prev if kc == 0 else 0))
            h2t[t] = (P.hsem[sl], P.hctr[sl].v)
        p8_act_done = P.sig("scalar")
        p8_pool_done = P.sig("gpsimd")
        P.wait("scalar", sv_gate)
        for yp in yn_tgts:
            P.wait_dma("scalar", yp)
        for gx in range(8):
            base = ring_use(18 + gx)
            last = None
            for i in range(4):
                cc = gx * 4 + i
                b = cc % 2
                pb_wait(b)
                for t2 in range(TT):
                    if gx == 0 and i == 0:
                        P.wait_dma("tensor", h2t[t2])
                    for kc in range(NKC_DM):
                        args = (pb[b][:, t2 * 128:(t2 + 1) * 128],
                                warena[:, base + (i * NKC_DM + kc) * 128:
                                       base + (i * NKC_DM + kc + 1) * 128],
                                h2_fm[:, kc * 512 + t2 * 128: kc * 512 + (t2 + 1) * 128])
                        if kc < NKC_DM - 1 or t2 < TT - 1:
                            mm(*args, start=(kc == 0), stop=(kc == NKC_DM - 1))
                        else:
                            last = mm_inc(*args, start=False, stop=True)
                P.wait("scalar", last)
                sa = a(lambda e, cc=cc, b=b: e.activation(
                    out=hg_fm[:, cc * T:(cc + 1) * T], in_=pb[b][:, 0:512],
                    func=AF.Gelu, bias=c_b1[:, cc:cc + 1], scale=0.5))
                pb_free(sa, b)
            ring_done(18 + gx, last)
            p9_pe_done = last

        p10sig = {}
        for gx in range(8):
            base = ring_use(26 + gx)
            if gx == 0:
                pb_wait(*range(8))
            last = None
            for t in range(TT):
                for i in range(4):
                    kc = gx * 4 + i
                    for b2 in range(2):
                        args = (pb[2 * t + b2][:, 0:512],
                                hg_fm[:, kc * T + t * 128: kc * T + (t + 1) * 128],
                                warena[:, base + i * 1024 + b2 * 512:
                                       base + i * 1024 + (b2 + 1) * 512])
                        start = (gx == 0 and i == 0)
                        stop = (gx == 7 and i == 3)
                        group_last = (t == TT - 1 and i == 3 and b2 == 1)
                        if stop or group_last:
                            s = mm_inc(*args, start=start, stop=stop)
                            if stop:
                                p10sig[t] = s
                            last = s
                        else:
                            mm(*args, start=start, stop=False)
            ring_done(26 + gx, last)
        if sc_i + 1 < total_sc:
            nseq = (sc_i + 1) // NSC_PER_SEQ
            ntok0 = ((sc_i + 1) % NSC_PER_SEQ) * T
            h1_done = emit_p1(nseq, ntok0, (h2t[TT - 1],), p9_pe_done, p8_pool_done)
        o_slots = (o_sb, stmp)
        f_slots = (f_sb, at_sb)
        sgo = {}
        for t in range(TT):
            sl = t % 2
            rsl = 2 + sl
            P.wait("vector", p10sig[t])
            if t >= 2:
                P.wait("vector", sgo[t - 2])
            sv_f = v(lambda e, t=t, sl=sl: e.scalar_tensor_tensor(
                out=f_slots[sl][:], in0=pbw[t][:], scalar=0.5, in1=c_b2[:],
                op0=ALU.mult, op1=ALU.add))
            pb_free(sv_f, 2 * t, 2 * t + 1)
            emit_ln_stats(f_slots[sl], rsl)
            sv_nm = P.sig("vector")
            P.wait("scalar", sv_nm)
            sa_ln = a(lambda e, sl=sl, rsl=rsl: e.activation(
                out=f_slots[sl][:], in_=f_slots[sl][:], func=AF.Identity,
                scale=rcol[:, 2 * rsl:2 * rsl + 1], bias=rcol[:, 2 * rsl + 1:2 * rsl + 2]))
            slot = o_slots[sl]
            P.wait("gpsimd", sa_ln)
            if t >= 2:
                tprev = P.stctr.v
                P.on("gpsimd", lambda e, tprev=tprev: e.wait_ge(P.stsem, tprev))
            sg_o = g(lambda e, t=t, slot=slot, sl=sl: e.tensor_add(
                out=slot[:], in0=x2_sb[:, t * 1024:(t + 1) * 1024], in1=f_slots[sl][:]))
            P.wait("gpsimd", sg_o)
            sgo[t] = sg_o
            P.store_out("gpsimd", out_d[seq, tok0 + t * 128: tok0 + (t + 1) * 128, :], slot[:])
        P.barrier()

    st_total = P.stctr.v

    def fin(e):
        e.wait_ge(P.stsem, st_total)
    P.on("gpsimd", fin)

    P.finalize_block()
    if dbg:
        build_program.taps = _dbg_state["taps"]
    return nc


def _host_prep(inputs):
    f32 = np.float32
    ip = np.ascontiguousarray(inputs["L0_attn_in_proj_w"], dtype=f32)
    wz = ip[0:DI]
    wxbc = ip[DI:DI + CD]
    wdt = ip[DI + CD:]
    wzdt_t = np.ascontiguousarray(np.concatenate([wz, wdt], axis=0).T)
    wxbc_t = np.ascontiguousarray(wxbc.T)
    conv_w = np.ascontiguousarray(inputs["L0_attn_conv_w"], dtype=f32)
    convdiag = np.zeros((NCC_XBC, 128, 4, 128), dtype=f32)
    for cc in range(NCC_XBC):
        for i in range(128):
            convdiag[cc, i, :, i] = conv_w[cc * 128 + i, :]
    convdiag = convdiag.reshape(NCC_XBC, 128, 4 * 128)
    convb_cols = np.ascontiguousarray(
        inputs["L0_attn_conv_b"].astype(f32).reshape(NCC_XBC, 128).T)
    rep = lambda vec, w: np.ascontiguousarray(
        np.broadcast_to(np.asarray(vec, f32).reshape(1, -1), (128, w)))
    dtb_rep = rep(inputs["L0_attn_dt_bias"], NH)
    alog_rep = rep(inputs["L0_attn_A_log"], NH)
    d_rep = rep(inputs["L0_attn_D"], NH)
    normw_rep = rep(inputs["L0_attn_norm_w"], DI)
    b1_cols = np.ascontiguousarray(inputs["L0_ffn_b1"].astype(f32).reshape(NCC_FF, 128).T)
    b2_rep = rep(inputs["L0_ffn_b2"], DM)
    wout_t = np.ascontiguousarray(inputs["L0_attn_out_proj_w"].astype(f32).T)
    w1_t = np.ascontiguousarray(inputs["L0_ffn_W1"].astype(f32).T)
    w2_t = np.ascontiguousarray(inputs["L0_ffn_W2"].astype(f32).T)
    ident = np.eye(128, dtype=f32)
    identb = np.eye(128, dtype=ml_dtypes.bfloat16)
    lt = np.triu(np.ones((128, 128), dtype=f32))
    return {
        "wzdt_t": wzdt_t, "wxbc_t": wxbc_t, "convdiag": convdiag,
        "convb_cols": convb_cols, "dtb_rep": dtb_rep, "alog_rep": alog_rep,
        "d_rep": d_rep, "normw_rep": normw_rep, "b1_cols": b1_cols,
        "b2_rep": b2_rep, "wout_t": wout_t, "w1_t": w1_t, "w2_t": w2_t,
        "ident128": ident, "ident128b": identb, "lt128": lt,
    }


def kernel(**inputs):
    x = np.ascontiguousarray(inputs["L0_attn_x"], dtype=np.float32)
    common = _host_prep(inputs)
    nc = build_program()
    in_maps = []
    for c in range(NCORES):
        m = dict(common)
        m["x_in"] = np.ascontiguousarray(x[c * SEQ_PER_CORE:(c + 1) * SEQ_PER_CORE])
        in_maps.append(m)
    res = run_bass_kernel_spmd(nc, in_maps, core_ids=list(range(NCORES)))
    out = np.concatenate([np.asarray(r["out"]) for r in res.results], axis=0)
    return out.astype(np.float32)
